# version 05
# baseline (speedup 1.0000x reference)
import contextlib
import numpy as np
import ml_dtypes
import concourse.bass as bass
import concourse.mybir as mybir
from concourse.bass_utils import run_bass_kernel_spmd

F32 = mybir.dt.float32
BF16 = mybir.dt.bfloat16
AF = mybir.ActivationFunctionType
ALU = mybir.AluOpType
AX = mybir.AxisListType.X

C = 512
NH = 16
DH = 128
D = 2048
NDT = 16
SCALE = 1.0 / float(np.sqrt(128.0))
EBIAS = -8.0
EPS = 1e-5
INV_D = 1.0 / 2048.0

RLO = [0, 0, 0, 0, 0, 128, 256, 384]
RHI = [128, 256, 384, 512, 512, 512, 512, 512]
TOFF = {3: 0, 0: 512, 2: 640, 4: 1024, 1: 1536, 6: 1792, 5: 2048, 7: 2432}
GROUPS = [([3, 0, 2], 0, 1024), ([4, 1, 6], 1024, 2048), ([5, 7], 2048, 2560)]
BOFF = {k: TOFF[k] + (128 * k if k < 4 else 128 * (k - 4)) - (RLO[k] if k >= 4 else 0)
        for k in range(8)}
BLO = {k: (128 * k if k < 4 else 128 * (k - 4)) for k in range(8)}

CONSTS = [0.0, 1.0, SCALE, EBIAS, -0.5, EPS, INV_D, -2.0]

USE_PARK = True


class Sem:
    def __init__(self, h):
        self.h = h
        self.n = 0

    def bump(self, by):
        self.n += by
        return self.n


def build_program(debug=False, nphase=99):
    nc = bass.Bass("TRN2", target_bir_lowering=False)

    xT = nc.dram_tensor("xT", [D, 6 * C], F32, kind="ExternalInput")
    wqkvT = nc.dram_tensor("wqkvT", [D, 3 * D], F32, kind="ExternalInput")
    woutT = nc.dram_tensor("woutT", [D, D], F32, kind="ExternalInput")
    w3T = nc.dram_tensor("w3T", [D, D], F32, kind="ExternalInput")
    swT = nc.dram_tensor("swT", [8 * D, C], F32, kind="ExternalInput")
    w1T = nc.dram_tensor("w1T", [128, 8 * 128], F32, kind="ExternalInput")
    w2T = nc.dram_tensor("w2T", [128, 8 * 128], F32, kind="ExternalInput")
    b1T = nc.dram_tensor("b1T", [128, 8], F32, kind="ExternalInput")
    b2T = nc.dram_tensor("b2T", [128, 8], F32, kind="ExternalInput")
    b3T = nc.dram_tensor("b3T", [128, 16], F32, kind="ExternalInput")
    swb = nc.dram_tensor("swb", [1, 8], F32, kind="ExternalInput")
    bsel = nc.dram_tensor("bsel", [1, 2], F32, kind="ExternalInput")
    maskR = nc.dram_tensor("maskR", [128, 2560], BF16, kind="ExternalInput")
    maskB = nc.dram_tensor("maskB", [128, 1024], BF16, kind="ExternalInput")
    cvals = nc.dram_tensor("cvals", [128, len(CONSTS)], F32, kind="ExternalInput")
    yT = nc.dram_tensor("yT", [D, 4 * C], F32, kind="ExternalOutput")

    cin = nc.dram_tensor("cc_in", [1, 16], F32)
    cout = nc.dram_tensor("cc_out", [1, 16], F32)
    if USE_PARK:
        wqkv_pk = nc.dram_tensor("wqkv_pk", [24 * NDT * 128, 256], BF16)
        wout_pk = nc.dram_tensor("wout_pk", [8 * NDT * 128, 256], BF16)
        w3_pk = nc.dram_tensor("w3_pk", [8 * NDT * 128, 256], BF16)

    dbg = {}
    if debug:
        dbg["x1f"] = nc.dram_tensor("dbg_x1f", [128, NDT * C], F32, kind="ExternalOutput")
        dbg["attnT"] = nc.dram_tensor("dbg_attnT", [128, NH * C], BF16, kind="ExternalOutput")
        dbg["logits"] = nc.dram_tensor("dbg_logits", [1, 64], F32, kind="ExternalOutput")
        dbg["q"] = nc.dram_tensor("dbg_q", [128, NH * C], BF16, kind="ExternalOutput")
        dbg["kT"] = nc.dram_tensor("dbg_kT", [128, NH * C], BF16, kind="ExternalOutput")
        dbg["v"] = nc.dram_tensor("dbg_v", [128, 4 * D], BF16, kind="ExternalOutput")
        dbg["acc"] = nc.dram_tensor("dbg_acc", [128, 24], F32, kind="ExternalOutput")
        dbg["pt"] = nc.dram_tensor("dbg_pt", [128, 2560], BF16, kind="ExternalOutput")
        dbg["rd"] = nc.dram_tensor("dbg_rd", [1, 512], F32, kind="ExternalOutput")
        dbg["bc1"] = nc.dram_tensor("dbg_bc1", [128, 512], F32, kind="ExternalOutput")

    st_ = contextlib.ExitStack()
    with st_:
        def sb(name, shape, dt):
            return st_.enter_context(nc.sbuf_tensor(name, shape, dt))

        def ps(name, shape):
            return st_.enter_context(nc.psum_tensor(name, shape, F32))

        s_cv = sb("s_cv", [128, len(CONSTS)], F32)
        s_ones1 = sb("s_ones1", [128, 1], BF16)
        s_ones1f = sb("s_ones1f", [128, 1], F32)
        s_onesr = sb("s_onesr", [1, 128], F32)
        s_maskB = sb("s_maskB", [128, 1024], BF16)
        s_b1T = sb("s_b1T", [128, 8], F32)
        s_b2T = sb("s_b2T", [128, 8], F32)
        s_b3T = sb("s_b3T", [128, 16], F32)
        s_w1T = sb("s_w1T", [128, 1024], BF16)
        s_w2T = sb("s_w2T", [128, 1024], BF16)
        s_swb = sb("s_swb", [1, 8], F32)
        s_bsel = sb("s_bsel", [1, 2], F32)
        s_g = sb("s_g", [1, 64], F32)
        s_lr = sb("s_lr", [1, 16], F32)
        s_contrib = sb("s_contrib", [1, 16], F32)
        s_ohb = sb("s_ohb", [128, 32], F32)
        s_bsel1 = sb("s_bsel1", [128, 4], F32)
        s_W1s = sb("s_W1s", [128, 256], BF16)
        s_W2s = sb("s_W2s", [128, 256], BF16)
        s_kT = [sb("s_kT0", [128, NH * C], BF16), sb("s_kT1", [128, NH * C], BF16)]
        s_v = [sb("s_v0", [128, 4 * D], BF16), sb("s_v1", [128, 4 * D], BF16)]
        M1 = sb("M1", [128, NDT * C], BF16)
        M2 = sb("M2", [128, NDT * C], BF16)
        M3 = sb("M3", [128, NDT * C], BF16)
        M4 = sb("M4", [128, NDT * C], F32)
        M5 = sb("M5", [128, NDT * 256], F32)
        s_wa = [sb("s_wa0", [128, NDT * 256], BF16), sb("s_wa1", [128, NDT * 256], BF16)]
        s_sws = [sb("s_sws0", [128, 2 * C], BF16), sb("s_sws1", [128, 2 * C], BF16)]
        s_pt = sb("s_pt", [128, 2560], BF16)
        s_pt2 = sb("s_pt2", [128, 2560], BF16)
        s_sq = [s_pt2[:, 0:C], s_pt2[:, C:2 * C]]
        s_sq_q = [s_pt2[:, 0:256], s_pt2[:, C:C + 256]]
        s_x32 = [sb("s_x32a", [128, C], F32), sb("s_x32b", [128, C], F32)]
        s_bc0 = s_x32[0]
        s_bc1 = s_x32[1]
        s_rd = sb("s_rd", [1, C], F32)
        s_h = sb("s_h", [128, 2 * C], BF16)
        s_sb = [s_h[:, 0:C], s_h[:, C:2 * C]]
        s_sb_q = [s_h[:, 0:256], s_h[:, C:C + 256]]
        s_acc = sb("s_acc", [128, 16], F32)
        s_accE = sb("s_accE", [128, 8], F32)

        for k, v_ in enumerate(CONSTS):
            nc.const_aps.register(F32, v_, s_cv[:, k:k + 1])

        pscA = ps("pscA", [128, 1024])
        pscB = ps("pscB", [128, 1024])
        pscC = ps("pscC", [128, 512])
        pPV = ps("pPV", [128, 512])
        pD = ps("pD", [128, 512])
        pX = ps("pX", [128, 512])
        RING = [pscA[:, 0:512], pscA[:, 512:1024], pscB[:, 0:512], pscB[:, 512:1024],
                pscC[:, :], pPV[:, :], pD[:, :], pX[:, :]]

        T = Sem(st_.enter_context(nc.semaphore("T")))
        S = Sem(st_.enter_context(nc.semaphore("S")))
        V = Sem(st_.enter_context(nc.semaphore("V")))
        dW = Sem(st_.enter_context(nc.semaphore("dW")))
        dX = Sem(st_.enter_context(nc.semaphore("dX")))
        dO = Sem(st_.enter_context(nc.semaphore("dO")))
        dR = Sem(st_.enter_context(nc.semaphore("dR")))
        dPk = Sem(st_.enter_context(nc.semaphore("dPk")))
        CCs = Sem(st_.enter_context(nc.semaphore("CCs")))

        events = []

        def E(eng, fn, waits=(), inc=None):
            wl = {}
            for (sm, val) in waits:
                assert val <= sm.n, f"forward wait {val} > {sm.n}"
                if val > 0:
                    wl[sm.h] = max(wl.get(sm.h, 0), val)
            incs = []
            if inc is not None:
                for (sm, by) in (inc if isinstance(inc, list) else [inc]):
                    sm.bump(by)
                    incs.append((sm.h, by))
            events.append((eng, list(wl.items()), fn, incs))

        def DMA(eng, out_ap, in_ap, sem, waits=()):
            E(eng, (lambda e, o=out_ap, i=in_ap: e.dma_start(out=o, in_=i)),
              waits, (sem, 16))
            return sem.n

        def MM(out_ap, lh, rh, start, stop, waits=(), inc=None):
            E("tensor", (lambda e, o=out_ap, l=lh, r=rh, a=start, b=stop:
                         e.matmul(o, l, r, start=a, stop=b, skip_group_check=True)),
              waits, inc)

        DMA("sync", s_cv[:, :], cvals.ap(), dX)
        DMA("sync", s_maskB[:, :], maskB.ap(), dX)
        DMA("sync", M3[:, 0:2560], maskR.ap(), dX)
        DMA("sync", s_b1T[:, :], b1T.ap(), dX)
        DMA("sync", s_b2T[:, :], b2T.ap(), dX)
        DMA("sync", s_b3T[:, :], b3T.ap(), dX)
        DMA("sync", s_swb[0:1, :], swb.ap(), dX)
        DMA("sync", s_bsel[0:1, :], bsel.ap(), dX)
        DMA("gpsimd", s_w1T[:, :], w1T.ap(), dW)
        DMA("gpsimd", s_w2T[:, :], w2T.ap(), dW)
        d_setup = dX.n
        dw_setup = dW.n
        E("vector", lambda e: e.memset(s_ones1[:, :], 1.0), [(dX, d_setup)], (V, 1))
        E("vector", lambda e: e.memset(s_ones1f[:, :], 1.0), (), (V, 1))
        E("vector", lambda e: e.memset(s_onesr[0:1, :], 1.0), (), (V, 1))
        v_ones = V.n

        xT_v = xT.ap().rearrange("(a p) t -> p a t", p=128)
        wqkvT_v = wqkvT.ap().rearrange("(a p) f -> p a f", p=128)
        woutT_v = woutT.ap().rearrange("(a p) f -> p a f", p=128)
        w3T_v = w3T.ap().rearrange("(a p) f -> p a f", p=128)
        swT_v = swT.ap().rearrange("(e a p) s -> e p a s", e=8, p=128)
        if USE_PARK:
            wqkv_pk_v = wqkv_pk.ap().rearrange("(f a p) t -> f p a t", a=NDT, p=128)
            wout_pk_v = wout_pk.ap().rearrange("(f a p) t -> f p a t", a=NDT, p=128)
            w3_pk_v = w3_pk.ap().rearrange("(f a p) t -> f p a t", a=NDT, p=128)
        yT_v = yT.ap().rearrange("(a p) t -> p a t", p=128)

        parked = {}

        wa_state = {"i": 0, "lastread": [(T, 0), (T, 0)]}

        PKV = {}

        def park_bg(kind, src_view, pk_view, fbs):
            for fb in fbs:
                DMA("gpsimd", pk_view[fb][:, 0:8, :],
                    src_view[:, 0:8, fb * 256:(fb + 1) * 256], dPk)
                DMA("gpsimd", pk_view[fb][:, 8:16, :],
                    src_view[:, 8:16, fb * 256:(fb + 1) * 256], dPk)
                parked[(kind, fb)] = dPk.n

        def load_wpiece(kind, fb, src_view, pk_view, pk_base):
            i = wa_state["i"]
            wa_state["i"] = 1 - i
            buf = s_wa[i]
            war = wa_state["lastread"][i]
            dest = buf.ap().rearrange("p (a t) -> p a t", a=NDT)
            key = (kind, fb)
            if key in parked:
                v0 = DMA("sync", dest[:, 0:8, :], pk_view[fb][:, 0:8, :], dW,
                         [war, (dPk, parked[key])])
                v1 = DMA("sync", dest[:, 8:16, :], pk_view[fb][:, 8:16, :], dW)
            else:
                v0 = DMA("gpsimd", dest[:, 0:8, :],
                         src_view[:, 0:8, fb * 256:(fb + 1) * 256], dW, [war])
                v1 = DMA("gpsimd", dest[:, 8:16, :],
                         src_view[:, 8:16, fb * 256:(fb + 1) * 256], dW)
            return buf, dW.n, i

        def wa_mark_read(i, val):
            wa_state["lastread"][i] = (T, val)

        ring_state = {"i": 0, "free": [(V, 0)] * 8}

        def ring_next(allowed=None):
            i = ring_state["i"]
            while allowed is not None and (i % 8) not in allowed:
                i = (i + 1) % 8
            ring_state["i"] = (i + 1) % 8
            return i, RING[i], ring_state["free"][i]

        def ring_set_free(i, sem_val):
            ring_state["free"][i] = sem_val

        x32_state = {"i": 0, "free": [(V, 0), (V, 0)]}
        st_state = {"i": 0, "free": [(T, 0), (T, 0)]}

        guard = {
            "M1": (T, 0), "M2": (T, 0), "M3": (T, 0), "M4": (V, 0), "M5": (dO, 0),
            "kT0": (T, 0), "kT1": (T, 0), "v0": (T, 0), "v1": (T, 0),
            "pt": (T, 0), "attnT_ready": (V, 0), "x1bf_ready": (S, 0),
            "x1f_ready": (V, 0), "q_ready": 0, "kv_ready": 0,
            "gates": (V, 0), "bc": (V, 0),
        }

        def qkv_phase(sp, kv_only, kvslot, extra_vec=None):
            m1v = M1.ap().rearrange("p (a t) -> p a t", a=NDT)
            w_m1 = guard["M1"]
            xv = []
            for q4 in range(4):
                xv.append(DMA("gpsimd", m1v[:, q4 * 4:(q4 + 1) * 4, :],
                              xT_v[:, q4 * 4:(q4 + 1) * 4, sp * C:(sp + 1) * C],
                              dX, [w_m1]))
            x_ready = dX.n

            fb_list = list(range(8, 24)) if kv_only else list(range(0, 24))
            for fb in fb_list:
                buf, wv, wi = load_wpiece("qkv", fb, wqkvT_v,
                                          wqkv_pk_v if USE_PARK else None, 0)
                last_t = None
                if fb < 16:
                    for sub in range(2):
                        ftile = fb * 2 + sub
                        ri, rap, rfree = ring_next()
                        for dt in range(NDT):
                            MM(rap, buf[:, dt * 256 + sub * 128: dt * 256 + sub * 128 + 128],
                               M1[:, dt * C:(dt + 1) * C],
                               start=(dt == 0), stop=(dt == NDT - 1),
                               waits=([(dW, wv), (dX, x_ready), rfree]
                                      if dt == 0 else ()),
                               inc=((T, 1) if dt == NDT - 1 else None))
                        t_done = T.n
                        if fb < 8:
                            dst = M2[:, (ftile % NH) * C:(ftile % NH) * C + C]
                            wdst = guard["M2"]
                        else:
                            dst = s_kT[kvslot][:, (ftile - 16) * C:(ftile - 16) * C + C]
                            wdst = guard["kT%d" % kvslot]
                        eng = "vector" if (ftile % 2 == 0) else "scalar"
                        sem = V if eng == "vector" else S
                        if eng == "vector":
                            E(eng, (lambda e, d=dst, p=rap: e.tensor_copy(d, p)),
                              [(T, t_done), wdst], (sem, 1))
                        else:
                            E(eng, (lambda e, d=dst, p=rap: e.copy(d, p)),
                              [(T, t_done), wdst], (sem, 1))
                        ring_set_free(ri, (sem, sem.n))
                        last_t = t_done
                        if extra_vec:
                            extra_vec.pop(0)()
                            if extra_vec and fb % 2 == 0:
                                extra_vec.pop(0)()
                else:
                    fo = (fb - 16) * 256
                    for tt in range(4):
                        ri, rap, rfree = ring_next()
                        for dt in range(NDT):
                            MM(rap[:, 0:256],
                               M1[:, dt * C + tt * 128: dt * C + tt * 128 + 128],
                               buf[:, dt * 256:(dt + 1) * 256],
                               start=(dt == 0), stop=(dt == NDT - 1),
                               waits=([(dW, wv), (dX, x_ready), rfree]
                                      if dt == 0 else ()),
                               inc=((T, 1) if dt == NDT - 1 else None))
                        t_done = T.n
                        dst = s_v[kvslot][:, tt * D + fo: tt * D + fo + 256]
                        eng = "vector" if (tt % 2 == 0) else "scalar"
                        sem = V if eng == "vector" else S
                        if eng == "vector":
                            E(eng, (lambda e, d=dst, p=rap[:, 0:256]: e.tensor_copy(d, p)),
                              [(T, t_done), guard["v%d" % kvslot]], (sem, 1))
                        else:
                            E(eng, (lambda e, d=dst, p=rap[:, 0:256]: e.copy(d, p)),
                              [(T, t_done), guard["v%d" % kvslot]], (sem, 1))
                        ring_set_free(ri, (sem, sem.n))
                        last_t = t_done
                        if extra_vec:
                            extra_vec.pop(0)()
                wa_mark_read(wi, last_t)
            guard["q_ready"] = [(S, S.n), (V, V.n)]
            guard["kv_ready"] = [(S, S.n), (V, V.n)]
            guard["M1_x"] = (T, T.n)
            if kv_only:
                guard["M1"] = (T, T.n)

        def attn_phase(kvslot, is_R, extra_vec):
            arena = {0: pscA, 1: pscB, 2: pscC}
            exp_done = [0, 0, 0]
            pvd_par = [[0, 0, 0], [0, 0, 0]]
            rf = ring_state["free"]
            bankfree = [rf[0], rf[2], rf[4]]
            bankfree2 = [rf[1], rf[3], rf[5]]
            pv_free = [rf[5], rf[7]]
            d_free = rf[6]
            qr = guard["q_ready"]
            norm_par = [0, 0]
            norm_done = 0
            s_bcv_prev = 0
            t_bc_prev = 0
            pts = [s_pt, s_pt2]

            for h in range(NH):
                par = h % 2
                pt = pts[par]
                pvb = [pPV, pX][par]
                vmasks = [0, 0, 0]
                for g, (jts, glo, ghi) in enumerate(GROUPS):
                    for idx, k in enumerate(jts):
                        lo, hi, toff = RLO[k], RHI[k], TOFF[k]
                        w = hi - lo
                        kt = s_kT[1 - kvslot] if k < 4 else s_kT[kvslot]
                        jj = k if k < 4 else k - 4
                        lhs = kt[:, h * C + jj * 128: h * C + jj * 128 + 128]
                        rhs = M2[:, h * C + lo: h * C + hi]
                        waits = []
                        if idx == 0:
                            waits = list(qr) + [bankfree[g], bankfree2[g],
                                                (S, exp_done[g])]
                        MM(arena[g][:, toff - glo: toff - glo + w], lhs, rhs,
                           start=True, stop=True, waits=waits,
                           inc=((T, 1) if idx == len(jts) - 1 else None))
                    t_sc = T.n
                    E("scalar",
                      (lambda e, a=arena[g][:, 0:(ghi - glo)], o=pt[:, glo:ghi]:
                       e.activation(o, a, AF.Exp, bias=EBIAS, scale=SCALE)),
                      [(T, t_sc), (T, pvd_par[par][g])], (S, 1))
                    exp_done[g] = S.n
                    if is_R:
                        E("vector",
                          (lambda e, o=pt[:, glo:ghi], m=M3[:, glo:ghi]:
                           e.tensor_tensor(o, o, m, ALU.mult)),
                          [(S, exp_done[g]), (dX, d_setup)], (V, 1))
                    else:
                        for k in jts:
                            bo = BOFF[k]
                            E("vector",
                              (lambda e, o=pt[:, bo:bo + 128],
                               m=s_maskB[:, k * 128:(k + 1) * 128]:
                               e.tensor_tensor(o, o, m, ALU.mult)),
                              [(S, exp_done[g]), (dX, d_setup)], (V, 1))
                    vmasks[g] = V.n
                for g, (jts, glo, ghi) in enumerate(GROUPS):
                    for k in jts:
                        lo, hi, toff = RLO[k], RHI[k], TOFF[k]
                        w = hi - lo
                        vb = s_v[1 - kvslot] if k < 4 else s_v[kvslot]
                        jj = k if k < 4 else k - 4
                        first = (g == 0 and k == jts[0])
                        lastk = (g == 2 and k == jts[-1])
                        waits = [(V, vmasks[g])]
                        if first:
                            waits += [pv_free[par], d_free, (S, s_bcv_prev)]
                        MM(pvb[:, lo:hi],
                           vb[:, jj * D + h * 128: jj * D + h * 128 + 128],
                           pt[:, toff:toff + w],
                           start=first, stop=lastk, waits=waits)
                        MM(pD[0:1, lo:hi], s_ones1[:, 0:1], pt[:, toff:toff + w],
                           start=first, stop=lastk,
                           inc=(T, 1) if (k == jts[-1]) else None)
                    pvd_par[par][g] = T.n
                t_pvd = T.n
                E("vector", (lambda e: e.reciprocal(s_rd[0:1, :], pD[0:1, :])),
                  [(T, t_pvd), (T, t_bc_prev)], (V, 1))
                v_rd = V.n
                MM(pD[:, 0:C], s_onesr[0:1, :], s_rd[0:1, :], start=True,
                   stop=True, waits=[(V, v_rd), (V, v_ones)], inc=(T, 1))
                t_bc_prev = T.n
                E("scalar", (lambda e: e.copy(s_bc1[:, :], pD[:, 0:C])),
                  [(T, t_bc_prev), (V, norm_done)] +
                  ([x32_state["free"][1]] if h == 0 else []), (S, 1))
                s_bcv_prev = S.n
                d_free = (S, s_bcv_prev)
                E("vector",
                  (lambda e, o=M1[:, h * C:(h + 1) * C], p=pvb:
                   e.tensor_tensor(o, p[:, :], s_bc1[:, :], ALU.mult)),
                  [(S, s_bcv_prev), (T, t_pvd)] +
                  ([guard["M1_x"]] if h == 0 else []),
                  (V, 1))
                norm_done = V.n
                norm_par[par] = V.n
                pv_free[par] = (V, V.n)
                if is_R and debug and h == 0:
                    DMA("sync", dbg["pt"].ap(), s_pt[:, :], dO, [(V, norm_done)])
                    DMA("sync", dbg["rd"].ap(), s_rd[0:1, :], dO, [(V, norm_done)])
                    DMA("sync", dbg["bc1"].ap(), s_bc1[:, :], dO, [(V, norm_done)])
                    dguard = (dO, dO.n)
                    pvd_par = [[0, 0, 0], [0, 0, 0]]
                    exp_done = [0, 0, 0]
                    E("scalar", None, [dguard])
                    E("vector", None, [dguard])
                for _ in range(6):
                    if extra_vec:
                        extra_vec.pop(0)()
            guard["attnT_ready"] = (V, V.n)
            x32_state["free"] = [(V, V.n), (V, V.n)]
            guard["M2"] = (T, T.n)
            guard["kT%d" % (1 - kvslot)] = (T, T.n)
            guard["v%d" % (1 - kvslot)] = (T, T.n)
            guard["kT%d" % kvslot] = (T, T.n)
            guard["v%d" % kvslot] = (T, T.n)
            guard["pt"] = (T, T.n)
            ring_state["free"] = [(S, exp_done[0]), (S, exp_done[0]),
                                  (S, exp_done[1]), (S, exp_done[1]),
                                  (S, exp_done[2]), (V, norm_par[0]),
                                  (S, s_bcv_prev), (V, norm_par[1])]

        def wout_ln1_phase(sp, x1bf_dst, is_R, gkey="M3"):
            atr = guard["attnT_ready"]
            m4w = guard["M4"]
            stat_mm = []
            for fb in range(8):
                buf, wv, wi = load_wpiece("wout", fb, woutT_v,
                                          wout_pk_v if USE_PARK else None, 0)
                last_t = None
                for sub in range(2):
                    fot = fb * 2 + sub
                    xi = x32_state["i"]
                    x32_state["i"] = 1 - xi
                    xv = DMA("sync", s_x32[xi][:, :],
                             xT_v[:, fot, sp * C:(sp + 1) * C],
                             dX, [x32_state["free"][xi]])
                    ri, rap, rfree = ring_next(allowed=(0, 1, 2, 3, 5, 6))
                    for ft in range(NDT):
                        MM(rap, buf[:, ft * 256 + sub * 128: ft * 256 + sub * 128 + 128],
                           M1[:, ft * C:(ft + 1) * C],
                           start=(ft == 0), stop=(ft == NDT - 1),
                           waits=([(dW, wv), atr, rfree] if ft == 0 else ()),
                           inc=((T, 1) if ft == NDT - 1 else None))
                    t_done = T.n
                    last_t = t_done
                    E("vector",
                      (lambda e, o=M4[:, fot * C:(fot + 1) * C], p=rap,
                       x=s_x32[xi][:, :]:
                       e.scalar_tensor_tensor(o, p, 1.0, x, ALU.bypass, ALU.add)),
                      [(T, t_done), (dX, xv), m4w], (V, 1))
                    v_z1 = V.n
                    x32_state["free"][xi] = (V, v_z1)
                    ring_set_free(ri, (V, v_z1))
                    si = st_state["i"]
                    st_state["i"] = 1 - si
                    E("vector",
                      (lambda e, o=s_sb[si], i_=M4[:, fot * C:(fot + 1) * C]:
                       e.tensor_copy(o, i_)),
                      [st_state["free"][si]], (V, 1))
                    v_sb = V.n
                    E("scalar",
                      (lambda e, o=s_sq[si], i_=M4[:, fot * C:(fot + 1) * C]:
                       e.activation(o, i_, AF.Square)),
                      [(V, v_z1), st_state["free"][si]] +
                      ([guard["pt"]] if fot == 0 else []), (S, 1))
                    s_sqv = S.n
                    w0 = [(V, v_sb)]
                    w1 = [(S, s_sqv)]
                    if fot == 0:
                        w0 += [ring_state["free"][4], (V, v_ones)]
                        w1 += [ring_state["free"][7]]
                    MM(pscC[0:1, 0:C], s_ones1[:, 0:1], s_sb[si],
                       start=(fot == 0), stop=(fot == NDT - 1), waits=w0)
                    MM(pX[0:1, 0:C], s_ones1[:, 0:1], s_sq[si],
                       start=(fot == 0), stop=(fot == NDT - 1), waits=w1,
                       inc=((T, 1) if fot == NDT - 1 else None))
                    st_state["free"][si] = (T, T.n)
                wa_mark_read(wi, last_t)
            t_stat = T.n
            E("vector", (lambda e: e.tensor_scalar_mul(s_bc0[0:1, 0:C],
                                                       pscC[0:1, 0:C], INV_D)),
              [(T, t_stat)], (V, 1))
            E("vector", (lambda e: e.tensor_scalar_mul(s_bc1[0:1, 0:C],
                                                       pX[0:1, 0:C], INV_D)),
              [(V, V.n)], (V, 1))
            E("vector", (lambda e: e.tensor_tensor(s_rd[0:1, 0:C],
                                                   s_bc0[0:1, 0:C],
                                                   s_bc0[0:1, 0:C], ALU.mult)),
              [(V, V.n)], (V, 1))
            E("vector", (lambda e: e.tensor_tensor(s_bc1[0:1, 0:C],
                                                   s_bc1[0:1, 0:C],
                                                   s_rd[0:1, 0:C],
                                                   ALU.subtract)),
              [(V, V.n)], (V, 1))
            v_var = V.n
            E("scalar", (lambda e: e.activation(s_rd[0:1, 0:C],
                                                s_bc1[0:1, 0:C],
                                                AF.Ln, bias=EPS)),
              [(V, v_var)], (S, 1))
            E("scalar", (lambda e: e.activation(s_bc1[0:1, 0:C],
                                                s_rd[0:1, 0:C],
                                                AF.Exp, scale=-0.5)),
              (), (S, 1))
            s_rstd = S.n
            MM(pscC[:, 0:C], s_onesr[0:1, :], s_bc0[0:1, 0:C], start=True,
               stop=True, waits=[(S, s_rstd)], inc=(T, 1))
            MM(pX[:, 0:C], s_onesr[0:1, :], s_bc1[0:1, 0:C], start=True,
               stop=True, inc=(T, 1))
            t_bc = T.n
            E("scalar", (lambda e: e.copy(s_bc0[:, :], pscC[:, 0:C])),
              [(T, t_bc)], (S, 1))
            E("scalar", (lambda e: e.copy(s_bc1[:, :], pX[:, 0:C])), (), (S, 1))
            s_bcv = S.n
            ring_state["free"][4] = (S, s_bcv)
            ring_state["free"][7] = (S, s_bcv)
            x1bf_w = guard[gkey]
            x1bf_vals = []
            for dt in range(NDT):
                E("vector",
                  (lambda e, z=M4[:, dt * C:(dt + 1) * C]:
                   e.tensor_tensor(z, z, s_bc0[:, :], ALU.subtract)),
                  [(S, s_bcv)] if dt == 0 else (), (V, 1))
                E("vector",
                  (lambda e, z=M4[:, dt * C:(dt + 1) * C]:
                   e.tensor_tensor(z, z, s_bc1[:, :], ALU.mult)),
                  (), (V, 1))
                v_ap = V.n
                E("scalar",
                  (lambda e, o=x1bf_dst[:, dt * C:(dt + 1) * C],
                   i_=M4[:, dt * C:(dt + 1) * C]: e.copy(o, i_)),
                  [(V, v_ap)] + ([x1bf_w] if dt == 0 else []), (S, 1))
                x1bf_vals.append(S.n)
            guard["x1f_ready"] = (V, V.n)
            guard["x1bf_ready"] = (S, S.n)
            guard["x1bf_vals"] = x1bf_vals
            guard["M1"] = (T, T.n)
            x32_state["free"] = [(V, V.n), (V, V.n)]

        def router_build(x1bf_src):
            state = {"vfree": [(V, 0), (V, 0)]}
            NPC = 64

            def queue_dma(pi):
                e_idx, q = pi // 8, pi % 8
                bi = pi % 2
                sv = s_sws[bi].ap().rearrange("p (a s) -> p a s", a=2)
                return DMA("gpsimd", sv,
                           swT_v[e_idx][:, q * 2:(q + 1) * 2, :],
                           dR, [state["vfree"][bi]])

            dmavals = [0] * NPC
            dmavals[0] = queue_dma(0)
            dmavals[1] = queue_dma(1)

            def make_piece(pi):
                e_idx, q = pi // 8, pi % 8
                bi = pi % 2

                def emit():
                    for j in range(2):
                        dtg = q * 2 + j
                        E("vector",
                          (lambda e, sw=s_sws[bi][:, j * C:(j + 1) * C],
                           x=x1bf_src[:, dtg * C:(dtg + 1) * C],
                           a=s_acc[:, dtg:dtg + 1]:
                           e.scalar_tensor_tensor(s_sb[0], sw, 1.0, x,
                                                  ALU.bypass, ALU.mult,
                                                  accum_out=a)),
                          [(dR, dmavals[pi]), guard["x1bf_ready"]] if j == 0
                          else (), (V, 1))
                    state["vfree"][bi] = (V, V.n)
                    if q == 7:
                        E("vector",
                          (lambda e, o=s_accE[:, e_idx:e_idx + 1]:
                           e.reduce_sum(o, s_acc[:, 0:16], axis=AX)),
                          [(V, V.n)], (V, 1))
                    if pi + 2 < NPC:
                        dmavals[pi + 2] = queue_dma(pi + 2)
                return emit

            return [make_piece(i) for i in range(NPC)]

        def router_finalize_a():
            v_acc = V.n
            MM(pX[0:1, 0:8], s_ones1f[:, 0:1], s_accE[:, 0:8], start=True,
               stop=True, waits=[(V, v_acc), ring_state["free"][7]], inc=(T, 1))
            t_par = T.n
            E("vector", (lambda e: e.tensor_scalar(s_contrib[0:1, 0:8],
                                                   pX[0:1, 0:8],
                                                   s_bsel[0:1, 0:1], None,
                                                   ALU.mult)),
              [(T, t_par), (V, V.n)], (V, 1))
            E("vector", (lambda e: e.tensor_scalar(s_contrib[0:1, 8:16],
                                                   pX[0:1, 0:8],
                                                   s_bsel[0:1, 1:2], None,
                                                   ALU.mult)),
              [(V, V.n)], (V, 1))
            v_ct = V.n
            ring_state["free"][7] = (V, v_ct)
            DMA("gpsimd", cin.ap(), s_contrib[0:1, :], dR, [(V, v_ct)])
            d_ct = dR.n
            E("gpsimd",
              (lambda e: e.collective_compute(
                  "AllReduce", ALU.add, replica_groups=[list(range(8))],
                  ins=[cin.ap()], outs=[cout.ap()]).then_inc(CCs.h)),
              [(dR, d_ct)], None)
            CCs.n = max(CCs.n, 1)
            DMA("sync", s_lr[0:1, :], cout.ap(), dR, [(CCs, 1)])
            d_lr = dR.n
            E("vector", (lambda e: e.tensor_scalar(s_g[0:1, 0:8], s_lr[0:1, 0:8],
                                                   s_bsel[0:1, 0:1], None,
                                                   ALU.mult)),
              [(dR, d_lr), (V, V.n)], (V, 1))
            E("vector", (lambda e: e.scalar_tensor_tensor(s_g[0:1, 0:8],
                                                          s_lr[0:1, 8:16],
                                                          s_bsel[0:1, 1:2],
                                                          s_g[0:1, 0:8],
                                                          ALU.mult, ALU.add)),
              [(V, V.n)], (V, 1))
            E("vector", (lambda e: e.tensor_tensor(s_g[0:1, 0:8], s_g[0:1, 0:8],
                                                   s_swb[0:1, 0:8], ALU.add)),
              [(V, V.n)], (V, 1))
            E("vector", (lambda e: e.reduce_max(s_g[0:1, 8:9], s_g[0:1, 0:8],
                                                axis=AX)),
              [(V, V.n)], (V, 1))
            E("vector", (lambda e: e.tensor_scalar_mul(s_g[0:1, 9:10],
                                                       s_g[0:1, 8:9], -1.0)),
              [(V, V.n)], (V, 1))
            v_gm = V.n
            E("scalar", (lambda e: e.activation(s_g[0:1, 16:24], s_g[0:1, 0:8],
                                                AF.Exp, bias=s_g[0:1, 9:10])),
              [(V, v_gm)], (S, 1))
            s_ex = S.n
            E("vector", (lambda e: e.reduce_sum(s_g[0:1, 24:25], s_g[0:1, 16:24],
                                                axis=AX)),
              [(S, s_ex), (V, V.n)], (V, 1))
            E("vector", (lambda e: e.reciprocal(s_g[0:1, 25:26], s_g[0:1, 24:25])),
              [(V, V.n)], (V, 1))
            E("vector", (lambda e: e.tensor_scalar(s_g[0:1, 16:24], s_g[0:1, 16:24],
                                                   s_g[0:1, 25:26], None,
                                                   ALU.mult)),
              [(V, V.n)], (V, 1))
            E("vector", (lambda e: e.reduce_max(s_g[0:1, 26:27], s_g[0:1, 16:24],
                                                axis=AX)),
              [(V, V.n)], (V, 1))
            E("vector", (lambda e: e.tensor_scalar(s_g[0:1, 32:40], s_g[0:1, 16:24],
                                                   s_g[0:1, 26:27], None,
                                                   ALU.is_ge)),
              [(V, V.n)], (V, 1))
            E("vector", (lambda e: e.scalar_tensor_tensor(s_g[0:1, 48:56],
                                                          s_g[0:1, 32:40], -2.0,
                                                          s_g[0:1, 16:24],
                                                          ALU.mult, ALU.add)),
              [(V, V.n)], (V, 1))
            E("vector", (lambda e: e.reduce_max(s_g[0:1, 27:28], s_g[0:1, 48:56],
                                                axis=AX)),
              [(V, V.n)], (V, 1))
            E("vector", (lambda e: e.tensor_scalar(s_g[0:1, 40:48], s_g[0:1, 48:56],
                                                   s_g[0:1, 27:28], None,
                                                   ALU.is_ge)),
              [(V, V.n)], (V, 1))
            E("vector", (lambda e: e.tensor_scalar(s_g[0:1, 48:56], s_g[0:1, 32:40],
                                                   s_g[0:1, 26:27], None,
                                                   ALU.mult)),
              [(V, V.n)], (V, 1))
            E("vector", (lambda e: e.tensor_scalar(s_g[0:1, 56:64], s_g[0:1, 40:48],
                                                   s_g[0:1, 27:28], None,
                                                   ALU.mult)),
              [(V, V.n)], (V, 1))
            guard["v_rows"] = V.n

        def router_finalize_b():
            v_rows = guard["v_rows"]
            MM(pX[:, 0:32], s_onesr[0:1, :], s_g[0:1, 32:64], start=True,
               stop=True, waits=[(V, v_rows), ring_state["free"][7]], inc=(T, 1))
            t_ohb = T.n
            E("scalar", (lambda e: e.copy(s_ohb[:, :], pX[:, 0:32])),
              [(T, t_ohb)], (S, 1))
            s_ohbv = S.n
            ring_state["free"][7] = (S, s_ohbv)
            for k in range(2):
                ohcol = (0 if k == 0 else 8)
                ggcol = (16 if k == 0 else 24)
                for e_ in range(8):
                    if e_ == 0:
                        E("vector",
                          (lambda e, o=s_W1s[:, k * 128:(k + 1) * 128],
                           w=s_w1T[:, e_ * 128:(e_ + 1) * 128],
                           c=s_ohb[:, ohcol + e_:ohcol + e_ + 1]:
                           e.tensor_scalar(o, w, c, None, ALU.mult)),
                          [(S, s_ohbv), (dW, dw_setup), (V, V.n)], (V, 1))
                        E("vector",
                          (lambda e, o=s_W2s[:, k * 128:(k + 1) * 128],
                           w=s_w2T[:, e_ * 128:(e_ + 1) * 128],
                           c=s_ohb[:, ggcol + e_:ggcol + e_ + 1]:
                           e.tensor_scalar(o, w, c, None, ALU.mult)),
                          [(V, V.n)], (V, 1))
                        E("vector",
                          (lambda e, o=s_bsel1[:, k:k + 1],
                           w=s_b1T[:, 0:1], c=s_ohb[:, ohcol:ohcol + 1]:
                           e.tensor_scalar(o, w, c, None, ALU.mult)),
                          [(V, V.n)], (V, 1))
                    else:
                        E("vector",
                          (lambda e, o=s_W1s[:, k * 128:(k + 1) * 128],
                           w=s_w1T[:, e_ * 128:(e_ + 1) * 128],
                           c=s_ohb[:, ohcol + e_:ohcol + e_ + 1]:
                           e.scalar_tensor_tensor(o, w, c, o, ALU.mult, ALU.add)),
                          [(V, V.n)], (V, 1))
                        E("vector",
                          (lambda e, o=s_W2s[:, k * 128:(k + 1) * 128],
                           w=s_w2T[:, e_ * 128:(e_ + 1) * 128],
                           c=s_ohb[:, ggcol + e_:ggcol + e_ + 1]:
                           e.scalar_tensor_tensor(o, w, c, o, ALU.mult, ALU.add)),
                          [(V, V.n)], (V, 1))
                        E("vector",
                          (lambda e, o=s_bsel1[:, k:k + 1],
                           w=s_b1T[:, e_:e_ + 1],
                           c=s_ohb[:, ohcol + e_:ohcol + e_ + 1]:
                           e.scalar_tensor_tensor(o, w, c, o, ALU.mult, ALU.add)),
                          [(V, V.n)], (V, 1))
            for e_ in range(8):
                if e_ == 0:
                    E("vector",
                      (lambda e, o=s_bsel1[:, 2:3], w=s_b2T[:, 0:1],
                       c=s_ohb[:, 16:17]:
                       e.tensor_scalar(o, w, c, None, ALU.mult)),
                      [(V, V.n)], (V, 1))
                else:
                    E("vector",
                      (lambda e, o=s_bsel1[:, 2:3], w=s_b2T[:, e_:e_ + 1],
                       c=s_ohb[:, 16 + e_:17 + e_]:
                       e.scalar_tensor_tensor(o, w, c, o, ALU.mult, ALU.add)),
                      [(V, V.n)], (V, 1))
                E("vector",
                  (lambda e, o=s_bsel1[:, 2:3], w=s_b2T[:, e_:e_ + 1],
                   c=s_ohb[:, 24 + e_:25 + e_]:
                   e.scalar_tensor_tensor(o, w, c, o, ALU.mult, ALU.add)),
                  [(V, V.n)], (V, 1))
            guard["gates"] = (V, V.n)
            if debug:
                DMA("sync", dbg["logits"].ap(), s_g[0:1, 0:64], dO,
                    [(V, V.n)])
                DMA("sync", dbg["acc"].ap()[:, 0:16], s_acc[:, :], dO, [(V, V.n)])
                DMA("sync", dbg["acc"].ap()[:, 16:24], s_accE[:, :], dO, [(V, V.n)])

        def ffn_phase(chunkpos, x1bf_src, gkey="M3"):
            gv = guard["gates"]
            x1bvs = guard.get("x1bf_vals", [guard["x1bf_ready"][1]] * NDT)
            zbf_done = [0] * NDT
            hg = guard.setdefault("hbuf", [(T, 0)])
            for ht in range(NDT):
                ri0, rap0, rf0 = ring_next()
                MM(rap0, s_W1s[:, 0:128], x1bf_src[:, ht * C:(ht + 1) * C],
                   start=True, stop=True,
                   waits=[gv, (S, x1bvs[ht]), rf0], inc=(T, 1))
                t0 = T.n
                ri1, rap1, rf1 = ring_next()
                MM(rap1, s_W1s[:, 128:256], x1bf_src[:, ht * C:(ht + 1) * C],
                   start=True, stop=True, waits=[rf1], inc=(T, 1))
                t1 = T.n
                h0 = s_h[:, 0:C]
                h1 = s_h[:, C:2 * C]
                E("scalar",
                  (lambda e, o=h0, p=rap0:
                   e.activation(o, p, AF.Gelu, bias=s_bsel1[:, 0:1])),
                  [(T, t0), hg[0]], (S, 1))
                E("scalar",
                  (lambda e, o=h1, p=rap1:
                   e.activation(o, p, AF.Gelu, bias=s_bsel1[:, 1:2])),
                  [(T, t1)], (S, 1))
                s_h_v = S.n
                ri2, rap2, rf2 = ring_next()
                MM(rap2, s_W2s[:, 0:128], h0, start=True, stop=False,
                   waits=[(S, s_h_v), rf2])
                MM(rap2, s_W2s[:, 128:256], h1, start=False, stop=True,
                   inc=(T, 1))
                t2 = T.n
                hg[0] = (T, t2)
                E("vector",
                  (lambda e, o=x1bf_src[:, ht * C:(ht + 1) * C], p=rap2,
                   x=x1bf_src[:, ht * C:(ht + 1) * C]:
                   e.scalar_tensor_tensor(o, p, s_bsel1[:, 2:3], x,
                                          ALU.add, ALU.add)),
                  [(T, t2)], (V, 1))
                zbf_done[ht] = V.n
                ring_set_free(ri0, (S, s_h_v))
                ring_set_free(ri1, (S, s_h_v))
                ring_set_free(ri2, (V, V.n))
            for half in range(2):
                stat_mm = []
                m5w = guard["M5"]
                for fb in range(8):
                    buf, wv, wi = load_wpiece("w3", fb, w3T_v,
                                              w3_pk_v if USE_PARK else None, 0)
                    last_t = None
                    for sub in range(2):
                        fot = fb * 2 + sub
                        ri, rap, rfree = ring_next(allowed=(0, 1, 2, 3, 5, 6))
                        for dt in range(NDT):
                            MM(rap[:, 0:256],
                               buf[:, dt * 256 + sub * 128: dt * 256 + sub * 128 + 128],
                               x1bf_src[:, dt * C + half * 256:
                                        dt * C + half * 256 + 256],
                               start=(dt == 0), stop=(dt == NDT - 1),
                               waits=([(dW, wv), (V, zbf_done[dt]), rfree]
                                      if dt == 0 else [(V, zbf_done[dt])]),
                               inc=((T, 1) if dt == NDT - 1 else None))
                        t_done = T.n
                        last_t = t_done
                        E("vector",
                          (lambda e, o=M5[:, fot * 256:(fot + 1) * 256],
                           p=rap[:, 0:256],
                           x=M4[:, fot * C + half * 256: fot * C + half * 256 + 256],
                           b=s_b3T[:, fot:fot + 1]:
                           e.scalar_tensor_tensor(o, p, b, x, ALU.add, ALU.add)),
                          [(T, t_done), guard["x1f_ready"]] +
                          ([m5w] if fot == 0 else []), (V, 1))
                        v_z2 = V.n
                        ring_set_free(ri, (V, v_z2))
                        si = st_state["i"]
                        st_state["i"] = 1 - si
                        E("vector",
                          (lambda e, o=s_sb_q[si],
                           i_=M5[:, fot * 256:(fot + 1) * 256]:
                           e.tensor_copy(o, i_)),
                          [st_state["free"][si]], (V, 1))
                        v_sb = V.n
                        E("scalar",
                          (lambda e, o=s_sq_q[si],
                           i_=M5[:, fot * 256:(fot + 1) * 256]:
                           e.activation(o, i_, AF.Square)),
                          [(V, v_z2), st_state["free"][si]], (S, 1))
                        s_sqv = S.n
                        w0 = [(V, v_sb)]
                        w1 = [(S, s_sqv)]
                        if fot == 0:
                            w0 += [ring_state["free"][4]]
                            w1 += [ring_state["free"][7]]
                        MM(pscC[0:1, 0:256], s_ones1[:, 0:1], s_sb_q[si],
                           start=(fot == 0), stop=(fot == NDT - 1), waits=w0)
                        MM(pX[0:1, 0:256], s_ones1[:, 0:1], s_sq_q[si],
                           start=(fot == 0), stop=(fot == NDT - 1), waits=w1,
                           inc=((T, 1) if fot == NDT - 1 else None))
                        st_state["free"][si] = (T, T.n)
                    wa_mark_read(wi, last_t)
                t_stat = T.n
                E("vector", (lambda e: e.tensor_scalar_mul(s_bc0[0:1, 0:256],
                                                           pscC[0:1, 0:256],
                                                           INV_D)),
                  [(T, t_stat)], (V, 1))
                E("vector", (lambda e: e.tensor_scalar_mul(s_bc1[0:1, 0:256],
                                                           pX[0:1, 0:256], INV_D)),
                  [(V, V.n)], (V, 1))
                E("vector", (lambda e: e.tensor_tensor(s_rd[0:1, 0:256],
                                                       s_bc0[0:1, 0:256],
                                                       s_bc0[0:1, 0:256],
                                                       ALU.mult)),
                  [(V, V.n)], (V, 1))
                E("vector", (lambda e: e.tensor_tensor(s_bc1[0:1, 0:256],
                                                       s_bc1[0:1, 0:256],
                                                       s_rd[0:1, 0:256],
                                                       ALU.subtract)),
                  [(V, V.n)], (V, 1))
                v_var = V.n
                E("scalar", (lambda e: e.activation(s_rd[0:1, 0:256],
                                                    s_bc1[0:1, 0:256],
                                                    AF.Ln, bias=EPS)),
                  [(V, v_var)], (S, 1))
                E("scalar", (lambda e: e.activation(s_bc1[0:1, 0:256],
                                                    s_rd[0:1, 0:256],
                                                    AF.Exp, scale=-0.5)),
                  (), (S, 1))
                s_rstd = S.n
                MM(pscC[:, 0:256], s_onesr[0:1, :], s_bc0[0:1, 0:256],
                   start=True, stop=True, waits=[(S, s_rstd)], inc=(T, 1))
                MM(pX[:, 0:256], s_onesr[0:1, :], s_bc1[0:1, 0:256],
                   start=True, stop=True, inc=(T, 1))
                t_bc = T.n
                E("scalar", (lambda e: e.copy(s_bc0[:, 0:256], pscC[:, 0:256])),
                  [(T, t_bc)], (S, 1))
                E("scalar", (lambda e: e.copy(s_bc1[:, 0:256], pX[:, 0:256])),
                  (), (S, 1))
                s_bcv = S.n
                ring_state["free"][4] = (S, s_bcv)
                ring_state["free"][7] = (S, s_bcv)
                for dt in range(NDT):
                    E("vector",
                      (lambda e, z=M5[:, dt * 256:(dt + 1) * 256]:
                       e.tensor_tensor(z, z, s_bc0[:, 0:256], ALU.subtract)),
                      [(S, s_bcv)] if dt == 0 else (), (V, 1))
                    E("vector",
                      (lambda e, z=M5[:, dt * 256:(dt + 1) * 256]:
                       e.tensor_tensor(z, z, s_bc1[:, 0:256], ALU.mult)),
                      (), (V, 1))
                v_y = V.n
                x32_state["free"] = [(V, v_y), (V, v_y)]
                m5view = M5.ap().rearrange("p (a t) -> p a t", a=NDT)
                DMA("sync", yT_v[:, 0:8, chunkpos * C + half * 256:
                                 chunkpos * C + half * 256 + 256],
                    m5view[:, 0:8, :], dO, [(V, v_y)])
                DMA("sync", yT_v[:, 8:16, chunkpos * C + half * 256:
                                 chunkpos * C + half * 256 + 256],
                    m5view[:, 8:16, :], dO)
                guard["M5"] = (dO, dO.n)
            guard["M4"] = (V, V.n)
            guard[gkey] = (T, T.n)

        def _schedule(ck):
            qkv_phase(0, kv_only=True, kvslot=0)
            ck()
            qkv_phase(1, kv_only=False, kvslot=1)
            if debug:
                DMA("sync", dbg["q"].ap(), M2[:, :], dO, list(guard["q_ready"]))
                guard["M2"] = (dO, dO.n)
                DMA("sync", dbg["kT"].ap(), s_kT[1][:, :], dO, list(guard["kv_ready"]))
                guard["kT1"] = (dO, dO.n)
                DMA("sync", dbg["v"].ap(), s_v[1][:, :], dO, list(guard["kv_ready"]))
                guard["v1"] = (dO, dO.n)
            ck()
            attn_phase(kvslot=1, is_R=True, extra_vec=[])
            ck()
            guard["M3"] = (V, V.n)
            wout_ln1_phase(1, M3, is_R=True, gkey="M3")
            ck()
            if debug:
                DMA("sync", dbg["x1f"].ap(), M4[:, :], dO, [guard["x1f_ready"]])
                guard["M4"] = (dO, dO.n)
                DMA("sync", dbg["attnT"].ap(), M1[:, :], dO, [guard["attnT_ready"]])
                guard["M1"] = (dO, dO.n)
            park_bg("qkv", wqkvT_v, wqkv_pk_v, range(8, 24))
            park_bg("qkv", wqkvT_v, wqkv_pk_v, range(0, 8))
            park_bg("w3", w3T_v, w3_pk_v, range(0, 8))
            park_bg("wout", woutT_v, wout_pk_v, range(0, 8))
            rpieces = router_build(M3)
            qkv_phase(2, kv_only=True, kvslot=0)
            ck()
            qkv_phase(3, kv_only=False, kvslot=1, extra_vec=rpieces)
            ck()
            while rpieces:
                rpieces.pop(0)()
            attn_phase(kvslot=1, is_R=False, extra_vec=[])
            ck()
            router_finalize_a()
            router_finalize_b()
            ck()
            ffn_phase(0, M3, gkey="M3")
            ck()
            wout_ln1_phase(3, M2, is_R=False, gkey="M2")
            ck()
            ffn_phase(1, M2, gkey="M2")
            ck()
            qkv_phase(4, kv_only=False, kvslot=0)
            ck()
            attn_phase(kvslot=0, is_R=False, extra_vec=[])
            ck()
            wout_ln1_phase(4, M2, is_R=False, gkey="M2")
            ck()
            ffn_phase(2, M2, gkey="M2")
            ck()
            qkv_phase(5, kv_only=False, kvslot=1)
            ck()
            attn_phase(kvslot=1, is_R=False, extra_vec=[])
            ck()
            wout_ln1_phase(5, M2, is_R=False, gkey="M2")
            ck()
            ffn_phase(3, M2, gkey="M2")
            ck()

        class _Stop(Exception):
            pass

        _pc = [0]

        def ck():
            _pc[0] += 1
            if _pc[0] > nphase:
                raise _Stop()

        try:
            _schedule(ck)
        except _Stop:
            pass

        E("sync", None, [(dO, dO.n)])

        build_program._events = events
        block = st_.enter_context(nc.Block())

        def replay(name):
            def go(eng):
                lastw = {}
                for (en, waits, fn, incs) in events:
                    if en != name:
                        continue
                    for (h, v_) in waits:
                        if v_ > lastw.get(h, 0):
                            eng.wait_ge(h, v_)
                            lastw[h] = v_
                    if fn is not None:
                        ins = fn(eng)
                        for (h, by) in incs:
                            ins = ins.then_inc(h, by)
            return go

        block.tensor(replay("tensor"))
        block.scalar(replay("scalar"))
        block.vector(replay("vector"))
        block.gpsimd(replay("gpsimd"))
        block.sync(replay("sync"))

    return nc


def _band_mask(zero_prev):
    m = np.zeros((128, 2560), dtype=np.float32)
    for k in range(8):
        lo, hi, toff = RLO[k], RHI[k], TOFF[k]
        p = np.arange(128)[:, None]
        cc = np.arange(lo, hi)[None, :]
        j = 128 * k + p
        if k < 4:
            v = (cc <= j).astype(np.float32)
            if zero_prev:
                v[:] = 0.0
        else:
            v = (cc >= j - 512).astype(np.float32)
        m[:, toff:toff + hi - lo] = v
    return m.astype(ml_dtypes.bfloat16)


def _boundary_mask():
    m = np.zeros((128, 1024), dtype=np.float32)
    for k in range(8):
        p = np.arange(128)[:, None]
        cc = (BLO[k] + np.arange(128))[None, :]
        j = 128 * k + p
        if k < 4:
            v = (cc <= j).astype(np.float32)
        else:
            v = (cc >= j - 512).astype(np.float32)
        m[:, k * 128:(k + 1) * 128] = v
    return m.astype(ml_dtypes.bfloat16)


def make_inputs(c, L0_attn_x, L0_attn_w_qkv, L0_attn_w_out, L0_ffn_switch_w,
                L0_ffn_switch_b, L0_ffn_w1, L0_ffn_b1, L0_ffn_w2, L0_ffn_b2,
                L0_ffn_w3, L0_ffn_b3, shared):
    b, ct = c // 4, c % 4
    xb = np.asarray(L0_attn_x[b])
    seq = [ct - 1, ct, 3 * ct + 3, 3 * ct + 4, 3 * ct + 5, 3 * ct + 6]
    xTm = np.zeros((D, 6 * C), dtype=np.float32)
    for i, ch in enumerate(seq):
        if ch >= 0:
            xTm[:, i * C:(i + 1) * C] = xb[ch * C:(ch + 1) * C, :].T
    sw = np.asarray(L0_ffn_switch_w).reshape(8, 2048, 2048)
    swTm = np.ascontiguousarray(
        sw[:, ct * C:(ct + 1) * C, :].transpose(0, 2, 1)).reshape(8 * D, C)
    bselr = np.zeros((1, 2), dtype=np.float32)
    bselr[0, b] = 1.0
    d = dict(shared)
    d.update({
        "xT": np.ascontiguousarray(xTm),
        "swT": swTm.astype(np.float32),
        "bsel": bselr,
        "maskR": np.ascontiguousarray(_band_mask(zero_prev=(ct == 0))),
    })
    return d


def kernel(L0_attn_x, L0_attn_w_qkv, L0_attn_w_out, L0_ffn_switch_w,
           L0_ffn_switch_b, L0_ffn_w1, L0_ffn_b1, L0_ffn_w2, L0_ffn_b2,
           L0_ffn_w3, L0_ffn_b3):
    args = (L0_attn_x, L0_attn_w_qkv, L0_attn_w_out, L0_ffn_switch_w,
            L0_ffn_switch_b, L0_ffn_w1, L0_ffn_b1, L0_ffn_w2, L0_ffn_b2,
            L0_ffn_w3, L0_ffn_b3)
    wqkv = np.asarray(L0_attn_w_qkv)
    shared = {
        "wqkvT": np.ascontiguousarray(wqkv.T),
        "woutT": np.ascontiguousarray(np.asarray(L0_attn_w_out).T),
        "w3T": np.ascontiguousarray(np.asarray(L0_ffn_w3).T),
        "w1T": np.ascontiguousarray(
            np.concatenate([np.asarray(L0_ffn_w1)[e].T for e in range(8)], axis=1)),
        "w2T": np.ascontiguousarray(
            np.concatenate([np.asarray(L0_ffn_w2)[e].T for e in range(8)], axis=1)),
        "b1T": np.ascontiguousarray(np.asarray(L0_ffn_b1).T),
        "b2T": np.ascontiguousarray(np.asarray(L0_ffn_b2).T),
        "b3T": np.ascontiguousarray(np.asarray(L0_ffn_b3).reshape(16, 128).T),
        "swb": np.asarray(L0_ffn_switch_b).reshape(1, 8).astype(np.float32),
        "maskB": np.ascontiguousarray(_boundary_mask()),
        "cvals": np.tile(np.array(CONSTS, dtype=np.float32)[None, :], (128, 1)),
    }
    nc = build_program(debug=False)
    in_maps = [make_inputs(c, *args, shared=shared) for c in range(8)]
    res = run_bass_kernel_spmd(nc, in_maps, core_ids=list(range(8)))
    out = np.zeros((2, 8192, 2048), dtype=np.float32)
    for c in range(8):
        b, ct = c // 4, c % 4
        ymat = res.results[c]["yT"]
        owned = [ct, 3 * ct + 4, 3 * ct + 5, 3 * ct + 6]
        for i, ch in enumerate(owned):
            out[b, ch * C:(ch + 1) * C, :] = ymat[:, i * C:(i + 1) * C].T
    return out


# version 06
# speedup vs baseline: 1.1279x; 1.1279x over previous
import contextlib
import numpy as np
import ml_dtypes
import concourse.bass as bass
import concourse.mybir as mybir
from concourse.bass_utils import run_bass_kernel_spmd

F32 = mybir.dt.float32
BF16 = mybir.dt.bfloat16
AF = mybir.ActivationFunctionType
ALU = mybir.AluOpType
AX = mybir.AxisListType.X

C = 512
NH = 16
DH = 128
D = 2048
NDT = 16
SCALE = 1.0 / float(np.sqrt(128.0))
EBIAS = -8.0
EPS = 1e-5
INV_D = 1.0 / 2048.0

RLO = [0, 0, 0, 0, 0, 128, 256, 384]
RHI = [128, 256, 384, 512, 512, 512, 512, 512]
TOFF = {3: 0, 0: 512, 2: 640, 4: 1024, 1: 1536, 6: 1792, 5: 2048, 7: 2432}
GROUPS = [([3, 0, 2], 0, 1024), ([4, 1, 6], 1024, 2048), ([5, 7], 2048, 2560)]
BOFF = {k: TOFF[k] + (128 * k if k < 4 else 128 * (k - 4)) - (RLO[k] if k >= 4 else 0)
        for k in range(8)}
BLO = {k: (128 * k if k < 4 else 128 * (k - 4)) for k in range(8)}

CONSTS = [0.0, 1.0, SCALE, EBIAS, -0.5, EPS, INV_D, -2.0]

USE_PARK = True


class Sem:
    def __init__(self, h):
        self.h = h
        self.n = 0

    def bump(self, by):
        self.n += by
        return self.n


def build_program(debug=False, nphase=99):
    nc = bass.Bass("TRN2", target_bir_lowering=False)

    xT = nc.dram_tensor("xT", [D, 6 * C], F32, kind="ExternalInput")
    wqkvT = nc.dram_tensor("wqkvT", [D, 3 * D], F32, kind="ExternalInput")
    woutT = nc.dram_tensor("woutT", [D, D], F32, kind="ExternalInput")
    w3T = nc.dram_tensor("w3T", [D, D], F32, kind="ExternalInput")
    swT = nc.dram_tensor("swT", [8 * D, C], F32, kind="ExternalInput")
    w1T = nc.dram_tensor("w1T", [128, 8 * 128], F32, kind="ExternalInput")
    w2T = nc.dram_tensor("w2T", [128, 8 * 128], F32, kind="ExternalInput")
    b1T = nc.dram_tensor("b1T", [128, 8], F32, kind="ExternalInput")
    b2T = nc.dram_tensor("b2T", [128, 8], F32, kind="ExternalInput")
    b3T = nc.dram_tensor("b3T", [128, 16], F32, kind="ExternalInput")
    swb = nc.dram_tensor("swb", [1, 8], F32, kind="ExternalInput")
    bsel = nc.dram_tensor("bsel", [1, 2], F32, kind="ExternalInput")
    maskR = nc.dram_tensor("maskR", [128, 2560], BF16, kind="ExternalInput")
    maskB = nc.dram_tensor("maskB", [128, 1024], BF16, kind="ExternalInput")
    cvals = nc.dram_tensor("cvals", [128, len(CONSTS)], F32, kind="ExternalInput")
    yT = nc.dram_tensor("yT", [D, 4 * C], F32, kind="ExternalOutput")

    cin = nc.dram_tensor("cc_in", [1, 16], F32)
    cout = nc.dram_tensor("cc_out", [1, 16], F32)
    if USE_PARK:
        wqkv_pk = nc.dram_tensor("wqkv_pk", [24 * NDT * 128, 256], BF16)
        wout_pk = nc.dram_tensor("wout_pk", [8 * NDT * 128, 256], BF16)
        w3_pk = nc.dram_tensor("w3_pk", [8 * NDT * 128, 256], BF16)

    dbg = {}
    if debug:
        dbg["x1f"] = nc.dram_tensor("dbg_x1f", [128, NDT * C], F32, kind="ExternalOutput")
        dbg["attnT"] = nc.dram_tensor("dbg_attnT", [128, NH * C], BF16, kind="ExternalOutput")
        dbg["logits"] = nc.dram_tensor("dbg_logits", [1, 64], F32, kind="ExternalOutput")
        dbg["q"] = nc.dram_tensor("dbg_q", [128, NH * C], BF16, kind="ExternalOutput")
        dbg["kT"] = nc.dram_tensor("dbg_kT", [128, NH * C], BF16, kind="ExternalOutput")
        dbg["v"] = nc.dram_tensor("dbg_v", [128, 4 * D], BF16, kind="ExternalOutput")
        dbg["acc"] = nc.dram_tensor("dbg_acc", [128, 24], F32, kind="ExternalOutput")
        dbg["pt"] = nc.dram_tensor("dbg_pt", [128, 2560], BF16, kind="ExternalOutput")
        dbg["rd"] = nc.dram_tensor("dbg_rd", [1, 512], F32, kind="ExternalOutput")
        dbg["bc1"] = nc.dram_tensor("dbg_bc1", [128, 512], F32, kind="ExternalOutput")

    st_ = contextlib.ExitStack()
    with st_:
        def sb(name, shape, dt):
            return st_.enter_context(nc.sbuf_tensor(name, shape, dt))

        def ps(name, shape):
            return st_.enter_context(nc.psum_tensor(name, shape, F32))

        s_cv = sb("s_cv", [128, len(CONSTS)], F32)
        s_ones1 = sb("s_ones1", [128, 1], BF16)
        s_ones1f = sb("s_ones1f", [128, 1], F32)
        s_onesr = sb("s_onesr", [1, 128], F32)
        s_maskB = sb("s_maskB", [128, 1024], BF16)
        s_b1T = sb("s_b1T", [128, 8], F32)
        s_b2T = sb("s_b2T", [128, 8], F32)
        s_b3T = sb("s_b3T", [128, 16], F32)
        s_w1T = sb("s_w1T", [128, 1024], BF16)
        s_w2T = sb("s_w2T", [128, 1024], BF16)
        s_swb = sb("s_swb", [1, 8], F32)
        s_bsel = sb("s_bsel", [1, 2], F32)
        s_g = sb("s_g", [1, 64], F32)
        s_lr = sb("s_lr", [1, 16], F32)
        s_contrib = sb("s_contrib", [1, 16], F32)
        s_ohb = sb("s_ohb", [128, 32], F32)
        s_bsel1 = sb("s_bsel1", [128, 4], F32)
        s_W1s = sb("s_W1s", [128, 256], BF16)
        s_W2s = sb("s_W2s", [128, 256], BF16)
        s_kT = [sb("s_kT0", [128, NH * C], BF16), sb("s_kT1", [128, NH * C], BF16)]
        s_v = [sb("s_v0", [128, 4 * D], BF16), sb("s_v1", [128, 4 * D], BF16)]
        M1 = sb("M1", [128, NDT * C], BF16)
        M2 = sb("M2", [128, NDT * C], BF16)
        M3 = sb("M3", [128, NDT * C], BF16)
        M4 = sb("M4", [128, NDT * C], F32)
        M5 = sb("M5", [128, NDT * 256], F32)
        s_wa = [sb("s_wa0", [128, NDT * 256], BF16), sb("s_wa1", [128, NDT * 256], BF16)]
        s_sws = [sb("s_sws0", [128, 2 * C], BF16), sb("s_sws1", [128, 2 * C], BF16)]
        s_pt = sb("s_pt", [128, 2560], BF16)
        s_pt2 = sb("s_pt2", [128, 2560], BF16)
        s_sq = [s_pt2[:, 0:C], s_pt2[:, C:2 * C]]
        s_sq_q = [s_pt2[:, 0:256], s_pt2[:, C:C + 256]]
        s_x32 = [sb("s_x32a", [128, C], F32), sb("s_x32b", [128, C], F32)]
        s_bc0 = s_x32[0]
        s_bc1 = s_x32[1]
        s_rd = sb("s_rd", [1, C], F32)
        s_h = sb("s_h", [128, 2 * C], BF16)
        s_sb = [s_h[:, 0:C], s_h[:, C:2 * C]]
        s_sb_q = [s_h[:, 0:256], s_h[:, C:C + 256]]
        s_acc = sb("s_acc", [128, 16], F32)
        s_accE = sb("s_accE", [128, 8], F32)

        for k, v_ in enumerate(CONSTS):
            nc.const_aps.register(F32, v_, s_cv[:, k:k + 1])

        pscA = ps("pscA", [128, 1024])
        pscB = ps("pscB", [128, 1024])
        pscC = ps("pscC", [128, 512])
        pPV = ps("pPV", [128, 512])
        pD = ps("pD", [128, 512])
        pX = ps("pX", [128, 512])
        RING = [pscA[:, 0:512], pscA[:, 512:1024], pscB[:, 0:512], pscB[:, 512:1024],
                pscC[:, :], pPV[:, :], pD[:, :], pX[:, :]]

        T = Sem(st_.enter_context(nc.semaphore("T")))
        S = Sem(st_.enter_context(nc.semaphore("S")))
        V = Sem(st_.enter_context(nc.semaphore("V")))
        dW = Sem(st_.enter_context(nc.semaphore("dW")))
        dX = Sem(st_.enter_context(nc.semaphore("dX")))
        dO = Sem(st_.enter_context(nc.semaphore("dO")))
        dR = Sem(st_.enter_context(nc.semaphore("dR")))
        dPk = Sem(st_.enter_context(nc.semaphore("dPk")))
        CCs = Sem(st_.enter_context(nc.semaphore("CCs")))

        events = []

        def E(eng, fn, waits=(), inc=None):
            wl = {}
            for (sm, val) in waits:
                assert val <= sm.n, f"forward wait {val} > {sm.n}"
                if val > 0:
                    wl[sm.h] = max(wl.get(sm.h, 0), val)
            incs = []
            if inc is not None:
                for (sm, by) in (inc if isinstance(inc, list) else [inc]):
                    sm.bump(by)
                    incs.append((sm.h, by))
            events.append((eng, list(wl.items()), fn, incs))

        def DMA(eng, out_ap, in_ap, sem, waits=()):
            E(eng, (lambda e, o=out_ap, i=in_ap: e.dma_start(out=o, in_=i)),
              waits, (sem, 16))
            return sem.n

        def MM(out_ap, lh, rh, start, stop, waits=(), inc=None):
            E("tensor", (lambda e, o=out_ap, l=lh, r=rh, a=start, b=stop:
                         e.matmul(o, l, r, start=a, stop=b, skip_group_check=True)),
              waits, inc)

        DMA("sync", s_cv[:, :], cvals.ap(), dX)
        DMA("sync", s_maskB[:, :], maskB.ap(), dX)
        DMA("sync", M3[:, 0:2560], maskR.ap(), dX)
        DMA("sync", s_b1T[:, :], b1T.ap(), dX)
        DMA("sync", s_b2T[:, :], b2T.ap(), dX)
        DMA("sync", s_b3T[:, :], b3T.ap(), dX)
        DMA("sync", s_swb[0:1, :], swb.ap(), dX)
        DMA("sync", s_bsel[0:1, :], bsel.ap(), dX)
        DMA("gpsimd", s_w1T[:, :], w1T.ap(), dW)
        DMA("gpsimd", s_w2T[:, :], w2T.ap(), dW)
        d_setup = dX.n
        dw_setup = dW.n
        E("vector", lambda e: e.memset(s_ones1[:, :], 1.0), [(dX, d_setup)], (V, 1))
        E("vector", lambda e: e.memset(s_ones1f[:, :], 1.0), (), (V, 1))
        E("vector", lambda e: e.memset(s_onesr[0:1, :], 1.0), (), (V, 1))
        v_ones = V.n

        xT_v = xT.ap().rearrange("(a p) t -> p a t", p=128)
        wqkvT_v = wqkvT.ap().rearrange("(a p) f -> p a f", p=128)
        woutT_v = woutT.ap().rearrange("(a p) f -> p a f", p=128)
        w3T_v = w3T.ap().rearrange("(a p) f -> p a f", p=128)
        swT_v = swT.ap().rearrange("(e a p) s -> e p a s", e=8, p=128)
        if USE_PARK:
            wqkv_pk_v = wqkv_pk.ap().rearrange("(f a p) t -> f p a t", a=NDT, p=128)
            wout_pk_v = wout_pk.ap().rearrange("(f a p) t -> f p a t", a=NDT, p=128)
            w3_pk_v = w3_pk.ap().rearrange("(f a p) t -> f p a t", a=NDT, p=128)
        yT_v = yT.ap().rearrange("(a p) t -> p a t", p=128)

        parked = {}

        wa_state = {"i": 0, "lastread": [(T, 0), (T, 0)]}

        PKV = {}

        def park_bg(kind, src_view, pk_view, fbs):
            for fb in fbs:
                DMA("gpsimd", pk_view[fb][:, 0:8, :],
                    src_view[:, 0:8, fb * 256:(fb + 1) * 256], dPk)
                DMA("gpsimd", pk_view[fb][:, 8:16, :],
                    src_view[:, 8:16, fb * 256:(fb + 1) * 256], dPk)
                parked[(kind, fb)] = dPk.n

        def load_wpiece(kind, fb, src_view, pk_view, pk_base):
            i = wa_state["i"]
            wa_state["i"] = 1 - i
            buf = s_wa[i]
            war = wa_state["lastread"][i]
            dest = buf.ap().rearrange("p (a t) -> p a t", a=NDT)
            key = (kind, fb)
            if key in parked:
                v0 = DMA("sync", dest[:, 0:8, :], pk_view[fb][:, 0:8, :], dW,
                         [war, (dPk, parked[key])])
                v1 = DMA("sync", dest[:, 8:16, :], pk_view[fb][:, 8:16, :], dW)
            else:
                v0 = DMA("gpsimd", dest[:, 0:8, :],
                         src_view[:, 0:8, fb * 256:(fb + 1) * 256], dW, [war])
                v1 = DMA("gpsimd", dest[:, 8:16, :],
                         src_view[:, 8:16, fb * 256:(fb + 1) * 256], dW)
            return buf, dW.n, i

        def wa_mark_read(i, val):
            wa_state["lastread"][i] = (T, val)

        ring_state = {"i": 0, "free": [(V, 0)] * 8}

        def ring_next(allowed=None):
            i = ring_state["i"]
            while allowed is not None and (i % 8) not in allowed:
                i = (i + 1) % 8
            ring_state["i"] = (i + 1) % 8
            return i, RING[i], ring_state["free"][i]

        def ring_set_free(i, sem_val):
            ring_state["free"][i] = sem_val

        x32_state = {"i": 0, "free": [(V, 0), (V, 0)]}
        st_state = {"i": 0, "free": [(T, 0), (T, 0)]}

        guard = {
            "M1": (T, 0), "M2": (T, 0), "M3": (T, 0), "M4": (V, 0), "M5": (dO, 0),
            "kT0": (T, 0), "kT1": (T, 0), "v0": (T, 0), "v1": (T, 0),
            "pt": (T, 0), "attnT_ready": (V, 0), "x1bf_ready": (S, 0),
            "x1f_ready": (V, 0), "q_ready": 0, "kv_ready": 0,
            "gates": (V, 0), "bc": (V, 0),
        }

        def xbf_load(sp):
            m1v = M1.ap().rearrange("p (a t) -> p a t", a=NDT)
            w_m1 = guard["M1"]
            for q4 in range(4):
                DMA("gpsimd", m1v[:, q4 * 4:(q4 + 1) * 4, :],
                    xT_v[:, q4 * 4:(q4 + 1) * 4, sp * C:(sp + 1) * C],
                    dX, [w_m1])
            return dX.n

        def qkv_phase(sp, kv_only, kvslot, extra_vec=None, x_ready=None):
            if x_ready is None:
                x_ready = xbf_load(sp)

            fb_list = list(range(8, 24)) if kv_only else list(range(0, 24))
            for fb in fb_list:
                buf, wv, wi = load_wpiece("qkv", fb, wqkvT_v,
                                          wqkv_pk_v if USE_PARK else None, 0)
                last_t = None
                if fb < 16:
                    for sub in range(2):
                        ftile = fb * 2 + sub
                        ri, rap, rfree = ring_next()
                        for dt in range(NDT):
                            MM(rap, buf[:, dt * 256 + sub * 128: dt * 256 + sub * 128 + 128],
                               M1[:, dt * C:(dt + 1) * C],
                               start=(dt == 0), stop=(dt == NDT - 1),
                               waits=([(dW, wv), (dX, x_ready), rfree]
                                      if dt == 0 else ()),
                               inc=((T, 1) if dt == NDT - 1 else None))
                        t_done = T.n
                        if fb < 8:
                            dst = M2[:, (ftile % NH) * C:(ftile % NH) * C + C]
                            wdst = guard["M2"]
                        else:
                            dst = s_kT[kvslot][:, (ftile - 16) * C:(ftile - 16) * C + C]
                            wdst = guard["kT%d" % kvslot]
                        eng = "vector" if (ftile % 2 == 0) else "scalar"
                        sem = V if eng == "vector" else S
                        if eng == "vector":
                            E(eng, (lambda e, d=dst, p=rap: e.tensor_copy(d, p)),
                              [(T, t_done), wdst], (sem, 1))
                        else:
                            E(eng, (lambda e, d=dst, p=rap: e.copy(d, p)),
                              [(T, t_done), wdst], (sem, 1))
                        ring_set_free(ri, (sem, sem.n))
                        last_t = t_done
                        if extra_vec:
                            extra_vec.pop(0)()
                            if extra_vec and fb % 2 == 0:
                                extra_vec.pop(0)()
                else:
                    fo = (fb - 16) * 256
                    for tt in range(4):
                        ri, rap, rfree = ring_next()
                        for dt in range(NDT):
                            MM(rap[:, 0:256],
                               M1[:, dt * C + tt * 128: dt * C + tt * 128 + 128],
                               buf[:, dt * 256:(dt + 1) * 256],
                               start=(dt == 0), stop=(dt == NDT - 1),
                               waits=([(dW, wv), (dX, x_ready), rfree]
                                      if dt == 0 else ()),
                               inc=((T, 1) if dt == NDT - 1 else None))
                        t_done = T.n
                        dst = s_v[kvslot][:, tt * D + fo: tt * D + fo + 256]
                        eng = "vector" if (tt % 2 == 0) else "scalar"
                        sem = V if eng == "vector" else S
                        if eng == "vector":
                            E(eng, (lambda e, d=dst, p=rap[:, 0:256]: e.tensor_copy(d, p)),
                              [(T, t_done), guard["v%d" % kvslot]], (sem, 1))
                        else:
                            E(eng, (lambda e, d=dst, p=rap[:, 0:256]: e.copy(d, p)),
                              [(T, t_done), guard["v%d" % kvslot]], (sem, 1))
                        ring_set_free(ri, (sem, sem.n))
                        last_t = t_done
                        if extra_vec:
                            extra_vec.pop(0)()
                wa_mark_read(wi, last_t)
            guard["q_ready"] = [(S, S.n), (V, V.n)]
            guard["kv_ready"] = [(S, S.n), (V, V.n)]
            guard["M1_x"] = (T, T.n)
            if kv_only:
                guard["M1"] = (T, T.n)

        def attn_phase(kvslot, is_R, extra_vec):
            arena = {0: pscA, 1: pscB, 2: pscC}
            exp_done = [0, 0, 0]
            pvd_par = [[0, 0, 0], [0, 0, 0]]
            rf = ring_state["free"]
            bankfree = [rf[0], rf[2], rf[4]]
            bankfree2 = [rf[1], rf[3], rf[5]]
            pv_free = [rf[5], rf[7]]
            d_free = rf[6]
            qr = guard["q_ready"]
            norm_par = [0, 0]
            norm_done = 0
            s_bcv_prev = 0
            t_bc_prev = 0
            pts = [s_pt, s_pt2]

            for h in range(NH):
                par = h % 2
                pt = pts[par]
                pvb = [pPV, pX][par]
                vmasks = [0, 0, 0]
                for g, (jts, glo, ghi) in enumerate(GROUPS):
                    for idx, k in enumerate(jts):
                        lo, hi, toff = RLO[k], RHI[k], TOFF[k]
                        w = hi - lo
                        kt = s_kT[1 - kvslot] if k < 4 else s_kT[kvslot]
                        jj = k if k < 4 else k - 4
                        lhs = kt[:, h * C + jj * 128: h * C + jj * 128 + 128]
                        rhs = M2[:, h * C + lo: h * C + hi]
                        waits = []
                        if idx == 0:
                            waits = list(qr) + [bankfree[g], bankfree2[g],
                                                (S, exp_done[g])]
                        MM(arena[g][:, toff - glo: toff - glo + w], lhs, rhs,
                           start=True, stop=True, waits=waits,
                           inc=((T, 1) if idx == len(jts) - 1 else None))
                    t_sc = T.n
                    E("scalar",
                      (lambda e, a=arena[g][:, 0:(ghi - glo)], o=pt[:, glo:ghi]:
                       e.activation(o, a, AF.Exp, bias=EBIAS, scale=SCALE)),
                      [(T, t_sc), (T, pvd_par[par][g])], (S, 1))
                    exp_done[g] = S.n
                    if is_R:
                        E("vector",
                          (lambda e, o=pt[:, glo:ghi], m=M3[:, glo:ghi]:
                           e.tensor_tensor(o, o, m, ALU.mult)),
                          [(S, exp_done[g]), (dX, d_setup)], (V, 1))
                    else:
                        for k in jts:
                            bo = BOFF[k]
                            E("vector",
                              (lambda e, o=pt[:, bo:bo + 128],
                               m=s_maskB[:, k * 128:(k + 1) * 128]:
                               e.tensor_tensor(o, o, m, ALU.mult)),
                              [(S, exp_done[g]), (dX, d_setup)], (V, 1))
                    vmasks[g] = V.n
                for g, (jts, glo, ghi) in enumerate(GROUPS):
                    for k in jts:
                        lo, hi, toff = RLO[k], RHI[k], TOFF[k]
                        w = hi - lo
                        vb = s_v[1 - kvslot] if k < 4 else s_v[kvslot]
                        jj = k if k < 4 else k - 4
                        first = (g == 0 and k == jts[0])
                        lastk = (g == 2 and k == jts[-1])
                        waits = [(V, vmasks[g])]
                        if first:
                            waits += [pv_free[par], d_free, (S, s_bcv_prev)]
                        MM(pvb[:, lo:hi],
                           vb[:, jj * D + h * 128: jj * D + h * 128 + 128],
                           pt[:, toff:toff + w],
                           start=first, stop=lastk, waits=waits)
                        MM(pD[0:1, lo:hi], s_ones1[:, 0:1], pt[:, toff:toff + w],
                           start=first, stop=lastk,
                           inc=(T, 1) if (k == jts[-1]) else None)
                    pvd_par[par][g] = T.n
                t_pvd = T.n
                E("vector", (lambda e: e.reciprocal(s_rd[0:1, :], pD[0:1, :])),
                  [(T, t_pvd), (T, t_bc_prev)], (V, 1))
                v_rd = V.n
                MM(pD[:, 0:C], s_onesr[0:1, :], s_rd[0:1, :], start=True,
                   stop=True, waits=[(V, v_rd), (V, v_ones)], inc=(T, 1))
                t_bc_prev = T.n
                E("scalar", (lambda e: e.copy(s_bc1[:, :], pD[:, 0:C])),
                  [(T, t_bc_prev), (V, norm_done)] +
                  ([x32_state["free"][1]] if h == 0 else []), (S, 1))
                s_bcv_prev = S.n
                d_free = (S, s_bcv_prev)
                E("vector",
                  (lambda e, o=M1[:, h * C:(h + 1) * C], p=pvb:
                   e.tensor_tensor(o, p[:, :], s_bc1[:, :], ALU.mult)),
                  [(S, s_bcv_prev), (T, t_pvd)] +
                  ([guard["M1_x"]] if h == 0 else []),
                  (V, 1))
                norm_done = V.n
                norm_par[par] = V.n
                pv_free[par] = (V, V.n)
                if is_R and debug and h == 0:
                    DMA("sync", dbg["pt"].ap(), s_pt[:, :], dO, [(V, norm_done)])
                    DMA("sync", dbg["rd"].ap(), s_rd[0:1, :], dO, [(V, norm_done)])
                    DMA("sync", dbg["bc1"].ap(), s_bc1[:, :], dO, [(V, norm_done)])
                    dguard = (dO, dO.n)
                    pvd_par = [[0, 0, 0], [0, 0, 0]]
                    exp_done = [0, 0, 0]
                    E("scalar", None, [dguard])
                    E("vector", None, [dguard])
                for _ in range(3):
                    if extra_vec:
                        extra_vec.pop(0)()
            guard["attnT_ready"] = (V, V.n)
            x32_state["free"] = [(V, V.n), (V, V.n)]
            guard["M2"] = (T, T.n)
            guard["kT%d" % (1 - kvslot)] = (T, T.n)
            guard["v%d" % (1 - kvslot)] = (T, T.n)
            guard["kT%d" % kvslot] = (T, T.n)
            guard["v%d" % kvslot] = (T, T.n)
            guard["pt"] = (T, T.n)
            ring_state["free"] = [(S, exp_done[0]), (S, exp_done[0]),
                                  (S, exp_done[1]), (S, exp_done[1]),
                                  (S, exp_done[2]), (V, norm_par[0]),
                                  (S, s_bcv_prev), (V, norm_par[1])]

        def wout_ln1_phase(sp, x1bf_dst, is_R, gkey="M3"):
            atr = guard["attnT_ready"]
            m4w = guard["M4"]
            stat_mm = []
            for fb in range(8):
                buf, wv, wi = load_wpiece("wout", fb, woutT_v,
                                          wout_pk_v if USE_PARK else None, 0)
                last_t = None
                for sub in range(2):
                    fot = fb * 2 + sub
                    xi = x32_state["i"]
                    x32_state["i"] = 1 - xi
                    xv = DMA("sync", s_x32[xi][:, :],
                             xT_v[:, fot, sp * C:(sp + 1) * C],
                             dX, [x32_state["free"][xi]])
                    ri, rap, rfree = ring_next(allowed=(0, 1, 2, 3, 5, 6))
                    for ft in range(NDT):
                        MM(rap, buf[:, ft * 256 + sub * 128: ft * 256 + sub * 128 + 128],
                           M1[:, ft * C:(ft + 1) * C],
                           start=(ft == 0), stop=(ft == NDT - 1),
                           waits=([(dW, wv), atr, rfree] if ft == 0 else ()),
                           inc=((T, 1) if ft == NDT - 1 else None))
                    t_done = T.n
                    last_t = t_done
                    E("vector",
                      (lambda e, o=M4[:, fot * C:(fot + 1) * C], p=rap,
                       x=s_x32[xi][:, :]:
                       e.scalar_tensor_tensor(o, p, 1.0, x, ALU.bypass, ALU.add)),
                      [(T, t_done), (dX, xv), m4w], (V, 1))
                    v_z1 = V.n
                    x32_state["free"][xi] = (V, v_z1)
                    ring_set_free(ri, (V, v_z1))
                    si = st_state["i"]
                    st_state["i"] = 1 - si
                    E("vector",
                      (lambda e, o=s_sb[si], i_=M4[:, fot * C:(fot + 1) * C]:
                       e.tensor_copy(o, i_)),
                      [st_state["free"][si]], (V, 1))
                    v_sb = V.n
                    E("scalar",
                      (lambda e, o=s_sq[si], i_=M4[:, fot * C:(fot + 1) * C]:
                       e.activation(o, i_, AF.Square)),
                      [(V, v_z1), st_state["free"][si]] +
                      ([guard["pt"]] if fot == 0 else []), (S, 1))
                    s_sqv = S.n
                    w0 = [(V, v_sb)]
                    w1 = [(S, s_sqv)]
                    if fot == 0:
                        w0 += [ring_state["free"][4], (V, v_ones)]
                        w1 += [ring_state["free"][7]]
                    MM(pscC[0:1, 0:C], s_ones1[:, 0:1], s_sb[si],
                       start=(fot == 0), stop=(fot == NDT - 1), waits=w0)
                    MM(pX[0:1, 0:C], s_ones1[:, 0:1], s_sq[si],
                       start=(fot == 0), stop=(fot == NDT - 1), waits=w1,
                       inc=((T, 1) if fot == NDT - 1 else None))
                    st_state["free"][si] = (T, T.n)
                wa_mark_read(wi, last_t)
            t_stat = T.n
            E("vector", (lambda e: e.tensor_scalar_mul(s_bc0[0:1, 0:C],
                                                       pscC[0:1, 0:C], INV_D)),
              [(T, t_stat)], (V, 1))
            E("vector", (lambda e: e.tensor_scalar_mul(s_bc1[0:1, 0:C],
                                                       pX[0:1, 0:C], INV_D)),
              [(V, V.n)], (V, 1))
            E("vector", (lambda e: e.tensor_tensor(s_rd[0:1, 0:C],
                                                   s_bc0[0:1, 0:C],
                                                   s_bc0[0:1, 0:C], ALU.mult)),
              [(V, V.n)], (V, 1))
            E("vector", (lambda e: e.tensor_tensor(s_bc1[0:1, 0:C],
                                                   s_bc1[0:1, 0:C],
                                                   s_rd[0:1, 0:C],
                                                   ALU.subtract)),
              [(V, V.n)], (V, 1))
            v_var = V.n
            E("scalar", (lambda e: e.activation(s_rd[0:1, 0:C],
                                                s_bc1[0:1, 0:C],
                                                AF.Ln, bias=EPS)),
              [(V, v_var)], (S, 1))
            E("scalar", (lambda e: e.activation(s_bc1[0:1, 0:C],
                                                s_rd[0:1, 0:C],
                                                AF.Exp, scale=-0.5)),
              (), (S, 1))
            s_rstd = S.n
            MM(pscC[:, 0:C], s_onesr[0:1, :], s_bc0[0:1, 0:C], start=True,
               stop=True, waits=[(S, s_rstd)], inc=(T, 1))
            MM(pX[:, 0:C], s_onesr[0:1, :], s_bc1[0:1, 0:C], start=True,
               stop=True, inc=(T, 1))
            t_bc = T.n
            E("scalar", (lambda e: e.copy(s_bc0[:, :], pscC[:, 0:C])),
              [(T, t_bc)], (S, 1))
            E("scalar", (lambda e: e.copy(s_bc1[:, :], pX[:, 0:C])), (), (S, 1))
            s_bcv = S.n
            ring_state["free"][4] = (S, s_bcv)
            ring_state["free"][7] = (S, s_bcv)
            x1bf_w = guard[gkey]
            x1bf_vals = []
            for dt in range(NDT):
                E("vector",
                  (lambda e, z=M4[:, dt * C:(dt + 1) * C]:
                   e.tensor_tensor(z, z, s_bc0[:, :], ALU.subtract)),
                  [(S, s_bcv)] if dt == 0 else (), (V, 1))
                E("vector",
                  (lambda e, z=M4[:, dt * C:(dt + 1) * C]:
                   e.tensor_tensor(z, z, s_bc1[:, :], ALU.mult)),
                  (), (V, 1))
                v_ap = V.n
                E("scalar",
                  (lambda e, o=x1bf_dst[:, dt * C:(dt + 1) * C],
                   i_=M4[:, dt * C:(dt + 1) * C]: e.copy(o, i_)),
                  [(V, v_ap)] + ([x1bf_w] if dt == 0 else []), (S, 1))
                x1bf_vals.append(S.n)
            guard["x1f_ready"] = (V, V.n)
            guard["x1bf_ready"] = (S, S.n)
            guard["x1bf_vals"] = x1bf_vals
            guard["M1"] = (T, T.n)
            x32_state["free"] = [(V, V.n), (V, V.n)]

        def router_build(x1bf_src):
            state = {"vfree": [(V, 0), (V, 0)]}
            NPC = 64

            def queue_dma(pi):
                e_idx, q = pi // 8, pi % 8
                bi = pi % 2
                sv = s_sws[bi].ap().rearrange("p (a s) -> p a s", a=2)
                return DMA("gpsimd", sv,
                           swT_v[e_idx][:, q * 2:(q + 1) * 2, :],
                           dR, [state["vfree"][bi]])

            dmavals = [0] * NPC
            dmavals[0] = queue_dma(0)
            dmavals[1] = queue_dma(1)

            def make_piece(pi):
                e_idx, q = pi // 8, pi % 8
                bi = pi % 2

                def emit():
                    for j in range(2):
                        dtg = q * 2 + j
                        E("vector",
                          (lambda e, sw=s_sws[bi][:, j * C:(j + 1) * C],
                           x=x1bf_src[:, dtg * C:(dtg + 1) * C],
                           a=s_acc[:, dtg:dtg + 1]:
                           e.scalar_tensor_tensor(s_sb[0], sw, 1.0, x,
                                                  ALU.bypass, ALU.mult,
                                                  accum_out=a)),
                          [(dR, dmavals[pi]), guard["x1bf_ready"]] if j == 0
                          else (), (V, 1))
                    state["vfree"][bi] = (V, V.n)
                    if q == 7:
                        E("vector",
                          (lambda e, o=s_accE[:, e_idx:e_idx + 1]:
                           e.reduce_sum(o, s_acc[:, 0:16], axis=AX)),
                          [(V, V.n)], (V, 1))
                    if pi + 2 < NPC:
                        dmavals[pi + 2] = queue_dma(pi + 2)
                return emit

            return [make_piece(i) for i in range(NPC)]

        def router_finalize_a():
            v_acc = V.n
            MM(pX[0:1, 0:8], s_ones1f[:, 0:1], s_accE[:, 0:8], start=True,
               stop=True, waits=[(V, v_acc), ring_state["free"][7]], inc=(T, 1))
            t_par = T.n
            E("vector", (lambda e: e.tensor_scalar(s_contrib[0:1, 0:8],
                                                   pX[0:1, 0:8],
                                                   s_bsel[0:1, 0:1], None,
                                                   ALU.mult)),
              [(T, t_par), (V, V.n)], (V, 1))
            E("vector", (lambda e: e.tensor_scalar(s_contrib[0:1, 8:16],
                                                   pX[0:1, 0:8],
                                                   s_bsel[0:1, 1:2], None,
                                                   ALU.mult)),
              [(V, V.n)], (V, 1))
            v_ct = V.n
            ring_state["free"][7] = (V, v_ct)
            DMA("gpsimd", cin.ap(), s_contrib[0:1, :], dR, [(V, v_ct)])
            d_ct = dR.n
            E("gpsimd",
              (lambda e: e.collective_compute(
                  "AllReduce", ALU.add, replica_groups=[list(range(8))],
                  ins=[cin.ap()], outs=[cout.ap()]).then_inc(CCs.h)),
              [(dR, d_ct)], None)
            CCs.n = max(CCs.n, 1)
            DMA("sync", s_lr[0:1, :], cout.ap(), dR, [(CCs, 1)])
            d_lr = dR.n
            E("vector", (lambda e: e.tensor_scalar(s_g[0:1, 0:8], s_lr[0:1, 0:8],
                                                   s_bsel[0:1, 0:1], None,
                                                   ALU.mult)),
              [(dR, d_lr), (V, V.n)], (V, 1))
            E("vector", (lambda e: e.scalar_tensor_tensor(s_g[0:1, 0:8],
                                                          s_lr[0:1, 8:16],
                                                          s_bsel[0:1, 1:2],
                                                          s_g[0:1, 0:8],
                                                          ALU.mult, ALU.add)),
              [(V, V.n)], (V, 1))
            E("vector", (lambda e: e.tensor_tensor(s_g[0:1, 0:8], s_g[0:1, 0:8],
                                                   s_swb[0:1, 0:8], ALU.add)),
              [(V, V.n)], (V, 1))
            E("vector", (lambda e: e.reduce_max(s_g[0:1, 8:9], s_g[0:1, 0:8],
                                                axis=AX)),
              [(V, V.n)], (V, 1))
            E("vector", (lambda e: e.tensor_scalar_mul(s_g[0:1, 9:10],
                                                       s_g[0:1, 8:9], -1.0)),
              [(V, V.n)], (V, 1))
            v_gm = V.n
            E("scalar", (lambda e: e.activation(s_g[0:1, 16:24], s_g[0:1, 0:8],
                                                AF.Exp, bias=s_g[0:1, 9:10])),
              [(V, v_gm)], (S, 1))
            s_ex = S.n
            E("vector", (lambda e: e.reduce_sum(s_g[0:1, 24:25], s_g[0:1, 16:24],
                                                axis=AX)),
              [(S, s_ex), (V, V.n)], (V, 1))
            E("vector", (lambda e: e.reciprocal(s_g[0:1, 25:26], s_g[0:1, 24:25])),
              [(V, V.n)], (V, 1))
            E("vector", (lambda e: e.tensor_scalar(s_g[0:1, 16:24], s_g[0:1, 16:24],
                                                   s_g[0:1, 25:26], None,
                                                   ALU.mult)),
              [(V, V.n)], (V, 1))
            E("vector", (lambda e: e.reduce_max(s_g[0:1, 26:27], s_g[0:1, 16:24],
                                                axis=AX)),
              [(V, V.n)], (V, 1))
            E("vector", (lambda e: e.tensor_scalar(s_g[0:1, 32:40], s_g[0:1, 16:24],
                                                   s_g[0:1, 26:27], None,
                                                   ALU.is_ge)),
              [(V, V.n)], (V, 1))
            E("vector", (lambda e: e.scalar_tensor_tensor(s_g[0:1, 48:56],
                                                          s_g[0:1, 32:40], -2.0,
                                                          s_g[0:1, 16:24],
                                                          ALU.mult, ALU.add)),
              [(V, V.n)], (V, 1))
            E("vector", (lambda e: e.reduce_max(s_g[0:1, 27:28], s_g[0:1, 48:56],
                                                axis=AX)),
              [(V, V.n)], (V, 1))
            E("vector", (lambda e: e.tensor_scalar(s_g[0:1, 40:48], s_g[0:1, 48:56],
                                                   s_g[0:1, 27:28], None,
                                                   ALU.is_ge)),
              [(V, V.n)], (V, 1))
            E("vector", (lambda e: e.tensor_scalar(s_g[0:1, 48:56], s_g[0:1, 32:40],
                                                   s_g[0:1, 26:27], None,
                                                   ALU.mult)),
              [(V, V.n)], (V, 1))
            E("vector", (lambda e: e.tensor_scalar(s_g[0:1, 56:64], s_g[0:1, 40:48],
                                                   s_g[0:1, 27:28], None,
                                                   ALU.mult)),
              [(V, V.n)], (V, 1))
            guard["v_rows"] = V.n

        def router_finalize_b():
            v_rows = guard["v_rows"]
            MM(pX[:, 0:32], s_onesr[0:1, :], s_g[0:1, 32:64], start=True,
               stop=True, waits=[(V, v_rows), ring_state["free"][7]], inc=(T, 1))
            t_ohb = T.n
            E("scalar", (lambda e: e.copy(s_ohb[:, :], pX[:, 0:32])),
              [(T, t_ohb)], (S, 1))
            s_ohbv = S.n
            ring_state["free"][7] = (S, s_ohbv)
            for k in range(2):
                ohcol = (0 if k == 0 else 8)
                ggcol = (16 if k == 0 else 24)
                for e_ in range(8):
                    if e_ == 0:
                        E("vector",
                          (lambda e, o=s_W1s[:, k * 128:(k + 1) * 128],
                           w=s_w1T[:, e_ * 128:(e_ + 1) * 128],
                           c=s_ohb[:, ohcol + e_:ohcol + e_ + 1]:
                           e.tensor_scalar(o, w, c, None, ALU.mult)),
                          [(S, s_ohbv), (dW, dw_setup), (V, V.n)], (V, 1))
                        E("vector",
                          (lambda e, o=s_W2s[:, k * 128:(k + 1) * 128],
                           w=s_w2T[:, e_ * 128:(e_ + 1) * 128],
                           c=s_ohb[:, ggcol + e_:ggcol + e_ + 1]:
                           e.tensor_scalar(o, w, c, None, ALU.mult)),
                          [(V, V.n)], (V, 1))
                        E("vector",
                          (lambda e, o=s_bsel1[:, k:k + 1],
                           w=s_b1T[:, 0:1], c=s_ohb[:, ohcol:ohcol + 1]:
                           e.tensor_scalar(o, w, c, None, ALU.mult)),
                          [(V, V.n)], (V, 1))
                    else:
                        E("vector",
                          (lambda e, o=s_W1s[:, k * 128:(k + 1) * 128],
                           w=s_w1T[:, e_ * 128:(e_ + 1) * 128],
                           c=s_ohb[:, ohcol + e_:ohcol + e_ + 1]:
                           e.scalar_tensor_tensor(o, w, c, o, ALU.mult, ALU.add)),
                          [(V, V.n)], (V, 1))
                        E("vector",
                          (lambda e, o=s_W2s[:, k * 128:(k + 1) * 128],
                           w=s_w2T[:, e_ * 128:(e_ + 1) * 128],
                           c=s_ohb[:, ggcol + e_:ggcol + e_ + 1]:
                           e.scalar_tensor_tensor(o, w, c, o, ALU.mult, ALU.add)),
                          [(V, V.n)], (V, 1))
                        E("vector",
                          (lambda e, o=s_bsel1[:, k:k + 1],
                           w=s_b1T[:, e_:e_ + 1],
                           c=s_ohb[:, ohcol + e_:ohcol + e_ + 1]:
                           e.scalar_tensor_tensor(o, w, c, o, ALU.mult, ALU.add)),
                          [(V, V.n)], (V, 1))
            for e_ in range(8):
                if e_ == 0:
                    E("vector",
                      (lambda e, o=s_bsel1[:, 2:3], w=s_b2T[:, 0:1],
                       c=s_ohb[:, 16:17]:
                       e.tensor_scalar(o, w, c, None, ALU.mult)),
                      [(V, V.n)], (V, 1))
                else:
                    E("vector",
                      (lambda e, o=s_bsel1[:, 2:3], w=s_b2T[:, e_:e_ + 1],
                       c=s_ohb[:, 16 + e_:17 + e_]:
                       e.scalar_tensor_tensor(o, w, c, o, ALU.mult, ALU.add)),
                      [(V, V.n)], (V, 1))
                E("vector",
                  (lambda e, o=s_bsel1[:, 2:3], w=s_b2T[:, e_:e_ + 1],
                   c=s_ohb[:, 24 + e_:25 + e_]:
                   e.scalar_tensor_tensor(o, w, c, o, ALU.mult, ALU.add)),
                  [(V, V.n)], (V, 1))
            guard["gates"] = (V, V.n)
            if debug:
                DMA("sync", dbg["logits"].ap(), s_g[0:1, 0:64], dO,
                    [(V, V.n)])
                DMA("sync", dbg["acc"].ap()[:, 0:16], s_acc[:, :], dO, [(V, V.n)])
                DMA("sync", dbg["acc"].ap()[:, 16:24], s_accE[:, :], dO, [(V, V.n)])

        def ffn_phase(chunkpos, x1bf_src, gkey="M3"):
            gv = guard["gates"]
            x1bvs = guard.get("x1bf_vals", [guard["x1bf_ready"][1]] * NDT)
            zbf_done = [0] * NDT
            hg = guard.setdefault("hbuf", [(T, 0)])
            for ht in range(NDT):
                ri0, rap0, rf0 = ring_next()
                MM(rap0, s_W1s[:, 0:128], x1bf_src[:, ht * C:(ht + 1) * C],
                   start=True, stop=True,
                   waits=[gv, (S, x1bvs[ht]), rf0], inc=(T, 1))
                t0 = T.n
                ri1, rap1, rf1 = ring_next()
                MM(rap1, s_W1s[:, 128:256], x1bf_src[:, ht * C:(ht + 1) * C],
                   start=True, stop=True, waits=[rf1], inc=(T, 1))
                t1 = T.n
                h0 = s_h[:, 0:C]
                h1 = s_h[:, C:2 * C]
                E("scalar",
                  (lambda e, o=h0, p=rap0:
                   e.activation(o, p, AF.Gelu, bias=s_bsel1[:, 0:1])),
                  [(T, t0), hg[0]], (S, 1))
                E("scalar",
                  (lambda e, o=h1, p=rap1:
                   e.activation(o, p, AF.Gelu, bias=s_bsel1[:, 1:2])),
                  [(T, t1)], (S, 1))
                s_h_v = S.n
                ri2, rap2, rf2 = ring_next()
                MM(rap2, s_W2s[:, 0:128], h0, start=True, stop=False,
                   waits=[(S, s_h_v), rf2])
                MM(rap2, s_W2s[:, 128:256], h1, start=False, stop=True,
                   inc=(T, 1))
                t2 = T.n
                hg[0] = (T, t2)
                E("vector",
                  (lambda e, o=x1bf_src[:, ht * C:(ht + 1) * C], p=rap2,
                   x=x1bf_src[:, ht * C:(ht + 1) * C]:
                   e.scalar_tensor_tensor(o, p, s_bsel1[:, 2:3], x,
                                          ALU.add, ALU.add)),
                  [(T, t2)], (V, 1))
                zbf_done[ht] = V.n
                ring_set_free(ri0, (S, s_h_v))
                ring_set_free(ri1, (S, s_h_v))
                ring_set_free(ri2, (V, V.n))
            for half in range(2):
                stat_mm = []
                m5w = guard["M5"]
                for fb in range(8):
                    buf, wv, wi = load_wpiece("w3", fb, w3T_v,
                                              w3_pk_v if USE_PARK else None, 0)
                    last_t = None
                    for sub in range(2):
                        fot = fb * 2 + sub
                        ri, rap, rfree = ring_next(allowed=(0, 1, 2, 3, 5, 6))
                        for dt in range(NDT):
                            MM(rap[:, 0:256],
                               buf[:, dt * 256 + sub * 128: dt * 256 + sub * 128 + 128],
                               x1bf_src[:, dt * C + half * 256:
                                        dt * C + half * 256 + 256],
                               start=(dt == 0), stop=(dt == NDT - 1),
                               waits=([(dW, wv), (V, zbf_done[dt]), rfree]
                                      if dt == 0 else [(V, zbf_done[dt])]),
                               inc=((T, 1) if dt == NDT - 1 else None))
                        t_done = T.n
                        last_t = t_done
                        E("vector",
                          (lambda e, o=M5[:, fot * 256:(fot + 1) * 256],
                           p=rap[:, 0:256],
                           x=M4[:, fot * C + half * 256: fot * C + half * 256 + 256],
                           b=s_b3T[:, fot:fot + 1]:
                           e.scalar_tensor_tensor(o, p, b, x, ALU.add, ALU.add)),
                          [(T, t_done), guard["x1f_ready"]] +
                          ([m5w] if fot == 0 else []), (V, 1))
                        v_z2 = V.n
                        ring_set_free(ri, (V, v_z2))
                        si = st_state["i"]
                        st_state["i"] = 1 - si
                        E("vector",
                          (lambda e, o=s_sb_q[si],
                           i_=M5[:, fot * 256:(fot + 1) * 256]:
                           e.tensor_copy(o, i_)),
                          [st_state["free"][si]], (V, 1))
                        v_sb = V.n
                        E("scalar",
                          (lambda e, o=s_sq_q[si],
                           i_=M5[:, fot * 256:(fot + 1) * 256]:
                           e.activation(o, i_, AF.Square)),
                          [(V, v_z2), st_state["free"][si]], (S, 1))
                        s_sqv = S.n
                        w0 = [(V, v_sb)]
                        w1 = [(S, s_sqv)]
                        if fot == 0:
                            w0 += [ring_state["free"][4]]
                            w1 += [ring_state["free"][7]]
                        MM(pscC[0:1, 0:256], s_ones1[:, 0:1], s_sb_q[si],
                           start=(fot == 0), stop=(fot == NDT - 1), waits=w0)
                        MM(pX[0:1, 0:256], s_ones1[:, 0:1], s_sq_q[si],
                           start=(fot == 0), stop=(fot == NDT - 1), waits=w1,
                           inc=((T, 1) if fot == NDT - 1 else None))
                        st_state["free"][si] = (T, T.n)
                    wa_mark_read(wi, last_t)
                t_stat = T.n
                E("vector", (lambda e: e.tensor_scalar_mul(s_bc0[0:1, 0:256],
                                                           pscC[0:1, 0:256],
                                                           INV_D)),
                  [(T, t_stat)], (V, 1))
                E("vector", (lambda e: e.tensor_scalar_mul(s_bc1[0:1, 0:256],
                                                           pX[0:1, 0:256], INV_D)),
                  [(V, V.n)], (V, 1))
                E("vector", (lambda e: e.tensor_tensor(s_rd[0:1, 0:256],
                                                       s_bc0[0:1, 0:256],
                                                       s_bc0[0:1, 0:256],
                                                       ALU.mult)),
                  [(V, V.n)], (V, 1))
                E("vector", (lambda e: e.tensor_tensor(s_bc1[0:1, 0:256],
                                                       s_bc1[0:1, 0:256],
                                                       s_rd[0:1, 0:256],
                                                       ALU.subtract)),
                  [(V, V.n)], (V, 1))
                v_var = V.n
                E("scalar", (lambda e: e.activation(s_rd[0:1, 0:256],
                                                    s_bc1[0:1, 0:256],
                                                    AF.Ln, bias=EPS)),
                  [(V, v_var)], (S, 1))
                E("scalar", (lambda e: e.activation(s_bc1[0:1, 0:256],
                                                    s_rd[0:1, 0:256],
                                                    AF.Exp, scale=-0.5)),
                  (), (S, 1))
                s_rstd = S.n
                MM(pscC[:, 0:256], s_onesr[0:1, :], s_bc0[0:1, 0:256],
                   start=True, stop=True, waits=[(S, s_rstd)], inc=(T, 1))
                MM(pX[:, 0:256], s_onesr[0:1, :], s_bc1[0:1, 0:256],
                   start=True, stop=True, inc=(T, 1))
                t_bc = T.n
                E("scalar", (lambda e: e.copy(s_bc0[:, 0:256], pscC[:, 0:256])),
                  [(T, t_bc)], (S, 1))
                E("scalar", (lambda e: e.copy(s_bc1[:, 0:256], pX[:, 0:256])),
                  (), (S, 1))
                s_bcv = S.n
                ring_state["free"][4] = (S, s_bcv)
                ring_state["free"][7] = (S, s_bcv)
                for dt in range(NDT):
                    E("vector",
                      (lambda e, z=M5[:, dt * 256:(dt + 1) * 256]:
                       e.tensor_tensor(z, z, s_bc0[:, 0:256], ALU.subtract)),
                      [(S, s_bcv)] if dt == 0 else (), (V, 1))
                    E("vector",
                      (lambda e, z=M5[:, dt * 256:(dt + 1) * 256]:
                       e.tensor_tensor(z, z, s_bc1[:, 0:256], ALU.mult)),
                      (), (V, 1))
                v_y = V.n
                x32_state["free"] = [(V, v_y), (V, v_y)]
                m5view = M5.ap().rearrange("p (a t) -> p a t", a=NDT)
                DMA("sync", yT_v[:, 0:8, chunkpos * C + half * 256:
                                 chunkpos * C + half * 256 + 256],
                    m5view[:, 0:8, :], dO, [(V, v_y)])
                DMA("sync", yT_v[:, 8:16, chunkpos * C + half * 256:
                                 chunkpos * C + half * 256 + 256],
                    m5view[:, 8:16, :], dO)
                guard["M5"] = (dO, dO.n)
            guard["M4"] = (V, V.n)
            guard[gkey] = (T, T.n)

        def _schedule(ck):
            qkv_phase(0, kv_only=True, kvslot=0)
            ck()
            qkv_phase(1, kv_only=False, kvslot=1)
            if debug:
                DMA("sync", dbg["q"].ap(), M2[:, :], dO, list(guard["q_ready"]))
                guard["M2"] = (dO, dO.n)
                DMA("sync", dbg["kT"].ap(), s_kT[1][:, :], dO, list(guard["kv_ready"]))
                guard["kT1"] = (dO, dO.n)
                DMA("sync", dbg["v"].ap(), s_v[1][:, :], dO, list(guard["kv_ready"]))
                guard["v1"] = (dO, dO.n)
            ck()
            attn_phase(kvslot=1, is_R=True, extra_vec=[])
            ck()
            guard["M3"] = (V, V.n)
            wout_ln1_phase(1, M3, is_R=True, gkey="M3")
            ck()
            if debug:
                DMA("sync", dbg["x1f"].ap(), M4[:, :], dO, [guard["x1f_ready"]])
                guard["M4"] = (dO, dO.n)
                DMA("sync", dbg["attnT"].ap(), M1[:, :], dO, [guard["attnT_ready"]])
                guard["M1"] = (dO, dO.n)
            park_bg("qkv", wqkvT_v, wqkv_pk_v, range(8, 24))
            x_h2 = xbf_load(2)
            park_bg("qkv", wqkvT_v, wqkv_pk_v, range(0, 8))
            park_bg("w3", w3T_v, w3_pk_v, range(0, 8))
            rpieces = router_build(M3)
            qkv_phase(2, kv_only=True, kvslot=0, x_ready=x_h2)
            ck()
            x_a0 = xbf_load(3)
            park_bg("wout", woutT_v, wout_pk_v, range(0, 8))
            qkv_phase(3, kv_only=False, kvslot=1, extra_vec=rpieces,
                      x_ready=x_a0)
            ck()
            attn_phase(kvslot=1, is_R=False, extra_vec=rpieces)
            ck()
            while rpieces:
                rpieces.pop(0)()
            router_finalize_a()
            router_finalize_b()
            ck()
            ffn_phase(0, M3, gkey="M3")
            ck()
            wout_ln1_phase(3, M2, is_R=False, gkey="M2")
            ck()
            ffn_phase(1, M2, gkey="M2")
            ck()
            qkv_phase(4, kv_only=False, kvslot=0)
            ck()
            attn_phase(kvslot=0, is_R=False, extra_vec=[])
            ck()
            wout_ln1_phase(4, M2, is_R=False, gkey="M2")
            ck()
            ffn_phase(2, M2, gkey="M2")
            ck()
            qkv_phase(5, kv_only=False, kvslot=1)
            ck()
            attn_phase(kvslot=1, is_R=False, extra_vec=[])
            ck()
            wout_ln1_phase(5, M2, is_R=False, gkey="M2")
            ck()
            ffn_phase(3, M2, gkey="M2")
            ck()

        class _Stop(Exception):
            pass

        _pc = [0]

        def ck():
            _pc[0] += 1
            if _pc[0] > nphase:
                raise _Stop()

        try:
            _schedule(ck)
        except _Stop:
            pass

        E("sync", None, [(dO, dO.n)])

        build_program._events = events
        block = st_.enter_context(nc.Block())

        def replay(name):
            def go(eng):
                lastw = {}
                for (en, waits, fn, incs) in events:
                    if en != name:
                        continue
                    for (h, v_) in waits:
                        if v_ > lastw.get(h, 0):
                            eng.wait_ge(h, v_)
                            lastw[h] = v_
                    if fn is not None:
                        ins = fn(eng)
                        for (h, by) in incs:
                            ins = ins.then_inc(h, by)
            return go

        block.tensor(replay("tensor"))
        block.scalar(replay("scalar"))
        block.vector(replay("vector"))
        block.gpsimd(replay("gpsimd"))
        block.sync(replay("sync"))

    return nc


def _band_mask(zero_prev):
    m = np.zeros((128, 2560), dtype=np.float32)
    for k in range(8):
        lo, hi, toff = RLO[k], RHI[k], TOFF[k]
        p = np.arange(128)[:, None]
        cc = np.arange(lo, hi)[None, :]
        j = 128 * k + p
        if k < 4:
            v = (cc <= j).astype(np.float32)
            if zero_prev:
                v[:] = 0.0
        else:
            v = (cc >= j - 512).astype(np.float32)
        m[:, toff:toff + hi - lo] = v
    return m.astype(ml_dtypes.bfloat16)


def _boundary_mask():
    m = np.zeros((128, 1024), dtype=np.float32)
    for k in range(8):
        p = np.arange(128)[:, None]
        cc = (BLO[k] + np.arange(128))[None, :]
        j = 128 * k + p
        if k < 4:
            v = (cc <= j).astype(np.float32)
        else:
            v = (cc >= j - 512).astype(np.float32)
        m[:, k * 128:(k + 1) * 128] = v
    return m.astype(ml_dtypes.bfloat16)


def make_inputs(c, L0_attn_x, L0_attn_w_qkv, L0_attn_w_out, L0_ffn_switch_w,
                L0_ffn_switch_b, L0_ffn_w1, L0_ffn_b1, L0_ffn_w2, L0_ffn_b2,
                L0_ffn_w3, L0_ffn_b3, shared):
    b, ct = c // 4, c % 4
    xb = np.asarray(L0_attn_x[b])
    seq = [ct - 1, ct, 3 * ct + 3, 3 * ct + 4, 3 * ct + 5, 3 * ct + 6]
    xTm = np.zeros((D, 6 * C), dtype=np.float32)
    for i, ch in enumerate(seq):
        if ch >= 0:
            xTm[:, i * C:(i + 1) * C] = xb[ch * C:(ch + 1) * C, :].T
    sw = np.asarray(L0_ffn_switch_w).reshape(8, 2048, 2048)
    swTm = np.ascontiguousarray(
        sw[:, ct * C:(ct + 1) * C, :].transpose(0, 2, 1)).reshape(8 * D, C)
    bselr = np.zeros((1, 2), dtype=np.float32)
    bselr[0, b] = 1.0
    d = dict(shared)
    d.update({
        "xT": np.ascontiguousarray(xTm),
        "swT": swTm.astype(np.float32),
        "bsel": bselr,
        "maskR": np.ascontiguousarray(_band_mask(zero_prev=(ct == 0))),
    })
    return d


def kernel(L0_attn_x, L0_attn_w_qkv, L0_attn_w_out, L0_ffn_switch_w,
           L0_ffn_switch_b, L0_ffn_w1, L0_ffn_b1, L0_ffn_w2, L0_ffn_b2,
           L0_ffn_w3, L0_ffn_b3):
    args = (L0_attn_x, L0_attn_w_qkv, L0_attn_w_out, L0_ffn_switch_w,
            L0_ffn_switch_b, L0_ffn_w1, L0_ffn_b1, L0_ffn_w2, L0_ffn_b2,
            L0_ffn_w3, L0_ffn_b3)
    wqkv = np.asarray(L0_attn_w_qkv)
    shared = {
        "wqkvT": np.ascontiguousarray(wqkv.T),
        "woutT": np.ascontiguousarray(np.asarray(L0_attn_w_out).T),
        "w3T": np.ascontiguousarray(np.asarray(L0_ffn_w3).T),
        "w1T": np.ascontiguousarray(
            np.concatenate([np.asarray(L0_ffn_w1)[e].T for e in range(8)], axis=1)),
        "w2T": np.ascontiguousarray(
            np.concatenate([np.asarray(L0_ffn_w2)[e].T for e in range(8)], axis=1)),
        "b1T": np.ascontiguousarray(np.asarray(L0_ffn_b1).T),
        "b2T": np.ascontiguousarray(np.asarray(L0_ffn_b2).T),
        "b3T": np.ascontiguousarray(np.asarray(L0_ffn_b3).reshape(16, 128).T),
        "swb": np.asarray(L0_ffn_switch_b).reshape(1, 8).astype(np.float32),
        "maskB": np.ascontiguousarray(_boundary_mask()),
        "cvals": np.tile(np.array(CONSTS, dtype=np.float32)[None, :], (128, 1)),
    }
    nc = build_program(debug=False)
    in_maps = [make_inputs(c, *args, shared=shared) for c in range(8)]
    res = run_bass_kernel_spmd(nc, in_maps, core_ids=list(range(8)))
    out = np.zeros((2, 8192, 2048), dtype=np.float32)
    for c in range(8):
        b, ct = c // 4, c % 4
        ymat = res.results[c]["yT"]
        owned = [ct, 3 * ct + 4, 3 * ct + 5, 3 * ct + 6]
        for i, ch in enumerate(owned):
            out[b, ch * C:(ch + 1) * C, :] = ymat[:, i * C:(i + 1) * C].T
    return out


# version 07
# speedup vs baseline: 1.1303x; 1.0021x over previous
import contextlib
import numpy as np
import ml_dtypes
import concourse.bass as bass
import concourse.mybir as mybir
from concourse.bass_utils import run_bass_kernel_spmd

F32 = mybir.dt.float32
BF16 = mybir.dt.bfloat16
AF = mybir.ActivationFunctionType
ALU = mybir.AluOpType
AX = mybir.AxisListType.X

C = 512
NH = 16
DH = 128
D = 2048
NDT = 16
SCALE = 1.0 / float(np.sqrt(128.0))
EBIAS = -8.0
EPS = 1e-5
INV_D = 1.0 / 2048.0

RLO = [0, 0, 0, 0, 0, 128, 256, 384]
RHI = [128, 256, 384, 512, 512, 512, 512, 512]
TOFF = {3: 0, 0: 512, 2: 640, 4: 1024, 1: 1536, 6: 1792, 5: 2048, 7: 2432}
GROUPS = [([3, 0, 2], 0, 1024), ([4, 1, 6], 1024, 2048), ([5, 7], 2048, 2560)]
BOFF = {k: TOFF[k] + (128 * k if k < 4 else 128 * (k - 4)) - (RLO[k] if k >= 4 else 0)
        for k in range(8)}
BLO = {k: (128 * k if k < 4 else 128 * (k - 4)) for k in range(8)}

CONSTS = [0.0, 1.0, SCALE, EBIAS, -0.5, EPS, INV_D, -2.0]

USE_PARK = True


class Sem:
    def __init__(self, h):
        self.h = h
        self.n = 0

    def bump(self, by):
        self.n += by
        return self.n


def build_program(debug=False, nphase=99):
    nc = bass.Bass("TRN2", target_bir_lowering=False)

    xT = nc.dram_tensor("xT", [D, 6 * C], F32, kind="ExternalInput")
    wqkvT = nc.dram_tensor("wqkvT", [D, 3 * D], F32, kind="ExternalInput")
    woutT = nc.dram_tensor("woutT", [D, D], F32, kind="ExternalInput")
    w3T = nc.dram_tensor("w3T", [D, D], F32, kind="ExternalInput")
    swT = nc.dram_tensor("swT", [8 * D, C], F32, kind="ExternalInput")
    w1T = nc.dram_tensor("w1T", [128, 8 * 128], F32, kind="ExternalInput")
    w2T = nc.dram_tensor("w2T", [128, 8 * 128], F32, kind="ExternalInput")
    b1T = nc.dram_tensor("b1T", [128, 8], F32, kind="ExternalInput")
    b2T = nc.dram_tensor("b2T", [128, 8], F32, kind="ExternalInput")
    b3T = nc.dram_tensor("b3T", [128, 16], F32, kind="ExternalInput")
    swb = nc.dram_tensor("swb", [1, 8], F32, kind="ExternalInput")
    bsel = nc.dram_tensor("bsel", [1, 2], F32, kind="ExternalInput")
    maskR = nc.dram_tensor("maskR", [128, 2560], BF16, kind="ExternalInput")
    maskB = nc.dram_tensor("maskB", [128, 1024], BF16, kind="ExternalInput")
    cvals = nc.dram_tensor("cvals", [128, len(CONSTS)], F32, kind="ExternalInput")
    yT = nc.dram_tensor("yT", [D, 4 * C], F32, kind="ExternalOutput")

    cin = nc.dram_tensor("cc_in", [1, 16], F32)
    cout = nc.dram_tensor("cc_out", [1, 16], F32)
    if USE_PARK:
        wqkv_pk = nc.dram_tensor("wqkv_pk", [24 * NDT * 128, 256], BF16)
        wout_pk = nc.dram_tensor("wout_pk", [8 * NDT * 128, 256], BF16)
        w3_pk = nc.dram_tensor("w3_pk", [8 * NDT * 128, 256], BF16)

    dbg = {}
    if debug:
        dbg["x1f"] = nc.dram_tensor("dbg_x1f", [128, NDT * C], F32, kind="ExternalOutput")
        dbg["attnT"] = nc.dram_tensor("dbg_attnT", [128, NH * C], BF16, kind="ExternalOutput")
        dbg["logits"] = nc.dram_tensor("dbg_logits", [1, 64], F32, kind="ExternalOutput")
        dbg["q"] = nc.dram_tensor("dbg_q", [128, NH * C], BF16, kind="ExternalOutput")
        dbg["kT"] = nc.dram_tensor("dbg_kT", [128, NH * C], BF16, kind="ExternalOutput")
        dbg["v"] = nc.dram_tensor("dbg_v", [128, 4 * D], BF16, kind="ExternalOutput")
        dbg["acc"] = nc.dram_tensor("dbg_acc", [128, 24], F32, kind="ExternalOutput")
        dbg["pt"] = nc.dram_tensor("dbg_pt", [128, 2560], BF16, kind="ExternalOutput")
        dbg["rd"] = nc.dram_tensor("dbg_rd", [1, 512], F32, kind="ExternalOutput")
        dbg["bc1"] = nc.dram_tensor("dbg_bc1", [128, 512], F32, kind="ExternalOutput")

    st_ = contextlib.ExitStack()
    with st_:
        def sb(name, shape, dt):
            return st_.enter_context(nc.sbuf_tensor(name, shape, dt))

        def ps(name, shape):
            return st_.enter_context(nc.psum_tensor(name, shape, F32))

        s_cv = sb("s_cv", [128, len(CONSTS)], F32)
        s_ones1 = sb("s_ones1", [128, 1], BF16)
        s_ones1f = sb("s_ones1f", [128, 1], F32)
        s_onesr = sb("s_onesr", [1, 128], F32)
        s_maskB = sb("s_maskB", [128, 1024], BF16)
        s_b1T = sb("s_b1T", [128, 8], F32)
        s_b2T = sb("s_b2T", [128, 8], F32)
        s_b3T = sb("s_b3T", [128, 16], F32)
        s_w1T = sb("s_w1T", [128, 1024], BF16)
        s_w2T = sb("s_w2T", [128, 1024], BF16)
        s_swb = sb("s_swb", [1, 8], F32)
        s_bsel = sb("s_bsel", [1, 2], F32)
        s_g = sb("s_g", [1, 64], F32)
        s_lr = sb("s_lr", [1, 16], F32)
        s_contrib = sb("s_contrib", [1, 16], F32)
        s_ohb = sb("s_ohb", [128, 32], F32)
        s_bsel1 = sb("s_bsel1", [128, 4], F32)
        s_W1s = sb("s_W1s", [128, 256], BF16)
        s_W2s = sb("s_W2s", [128, 256], BF16)
        s_kT = [sb("s_kT0", [128, NH * C], BF16), sb("s_kT1", [128, NH * C], BF16)]
        s_v = [sb("s_v0", [128, 4 * D], BF16), sb("s_v1", [128, 4 * D], BF16)]
        M1 = sb("M1", [128, NDT * C], BF16)
        M2 = sb("M2", [128, NDT * C], BF16)
        M3 = sb("M3", [128, NDT * C], BF16)
        M4 = sb("M4", [128, NDT * C], F32)
        M5 = sb("M5", [128, NDT * 256], F32)
        s_wa = [sb("s_wa0", [128, NDT * 256], BF16), sb("s_wa1", [128, NDT * 256], BF16)]
        s_sws = [sb("s_sws0", [128, 2 * C], BF16), sb("s_sws1", [128, 2 * C], BF16)]
        s_pt = sb("s_pt", [128, 2560], BF16)
        s_pt2 = sb("s_pt2", [128, 2560], BF16)
        s_sq = [s_pt2[:, 0:C], s_pt2[:, C:2 * C]]
        s_sq_q = [s_pt2[:, 0:256], s_pt2[:, C:C + 256]]
        s_x32 = [sb("s_x32a", [128, C], F32), sb("s_x32b", [128, C], F32)]
        s_bc0 = s_x32[0]
        s_bc1 = s_x32[1]
        s_rd = sb("s_rd", [1, C], F32)
        s_h = sb("s_h", [128, 2 * C], BF16)
        s_sb = [s_h[:, 0:C], s_h[:, C:2 * C]]
        s_sb_q = [s_h[:, 0:256], s_h[:, C:C + 256]]
        s_acc = sb("s_acc", [128, 16], F32)
        s_accE = sb("s_accE", [128, 8], F32)

        for k, v_ in enumerate(CONSTS):
            nc.const_aps.register(F32, v_, s_cv[:, k:k + 1])

        pscA = ps("pscA", [128, 1024])
        pscB = ps("pscB", [128, 1024])
        pscC = ps("pscC", [128, 512])
        pPV = ps("pPV", [128, 512])
        pD = ps("pD", [128, 512])
        pX = ps("pX", [128, 512])
        RING = [pscA[:, 0:512], pscA[:, 512:1024], pscB[:, 0:512], pscB[:, 512:1024],
                pscC[:, :], pPV[:, :], pD[:, :], pX[:, :]]

        T = Sem(st_.enter_context(nc.semaphore("T")))
        S = Sem(st_.enter_context(nc.semaphore("S")))
        V = Sem(st_.enter_context(nc.semaphore("V")))
        dW = Sem(st_.enter_context(nc.semaphore("dW")))
        dX = Sem(st_.enter_context(nc.semaphore("dX")))
        dO = Sem(st_.enter_context(nc.semaphore("dO")))
        dR = Sem(st_.enter_context(nc.semaphore("dR")))
        dPk = Sem(st_.enter_context(nc.semaphore("dPk")))
        CCs = Sem(st_.enter_context(nc.semaphore("CCs")))

        events = []

        def E(eng, fn, waits=(), inc=None):
            wl = {}
            for (sm, val) in waits:
                assert val <= sm.n, f"forward wait {val} > {sm.n}"
                if val > 0:
                    wl[sm.h] = max(wl.get(sm.h, 0), val)
            incs = []
            if inc is not None:
                for (sm, by) in (inc if isinstance(inc, list) else [inc]):
                    sm.bump(by)
                    incs.append((sm.h, by))
            events.append((eng, list(wl.items()), fn, incs))

        def DMA(eng, out_ap, in_ap, sem, waits=()):
            E(eng, (lambda e, o=out_ap, i=in_ap: e.dma_start(out=o, in_=i)),
              waits, (sem, 16))
            return sem.n

        def MM(out_ap, lh, rh, start, stop, waits=(), inc=None):
            E("tensor", (lambda e, o=out_ap, l=lh, r=rh, a=start, b=stop:
                         e.matmul(o, l, r, start=a, stop=b, skip_group_check=True)),
              waits, inc)

        DMA("sync", s_cv[:, :], cvals.ap(), dX)
        DMA("sync", s_maskB[:, :], maskB.ap(), dX)
        DMA("sync", M3[:, 0:2560], maskR.ap(), dX)
        DMA("sync", s_b1T[:, :], b1T.ap(), dX)
        DMA("sync", s_b2T[:, :], b2T.ap(), dX)
        DMA("sync", s_b3T[:, :], b3T.ap(), dX)
        DMA("sync", s_swb[0:1, :], swb.ap(), dX)
        DMA("sync", s_bsel[0:1, :], bsel.ap(), dX)
        DMA("gpsimd", s_w1T[:, :], w1T.ap(), dW)
        DMA("gpsimd", s_w2T[:, :], w2T.ap(), dW)
        d_setup = dX.n
        dw_setup = dW.n
        E("vector", lambda e: e.memset(s_ones1[:, :], 1.0), [(dX, d_setup)], (V, 1))
        E("vector", lambda e: e.memset(s_ones1f[:, :], 1.0), (), (V, 1))
        E("vector", lambda e: e.memset(s_onesr[0:1, :], 1.0), (), (V, 1))
        v_ones = V.n

        xT_v = xT.ap().rearrange("(a p) t -> p a t", p=128)
        wqkvT_v = wqkvT.ap().rearrange("(a p) f -> p a f", p=128)
        woutT_v = woutT.ap().rearrange("(a p) f -> p a f", p=128)
        w3T_v = w3T.ap().rearrange("(a p) f -> p a f", p=128)
        swT_v = swT.ap().rearrange("(e a p) s -> e p a s", e=8, p=128)
        if USE_PARK:
            wqkv_pk_v = wqkv_pk.ap().rearrange("(f a p) t -> f p a t", a=NDT, p=128)
            wout_pk_v = wout_pk.ap().rearrange("(f a p) t -> f p a t", a=NDT, p=128)
            w3_pk_v = w3_pk.ap().rearrange("(f a p) t -> f p a t", a=NDT, p=128)
        yT_v = yT.ap().rearrange("(a p) t -> p a t", p=128)

        parked = {}

        wa_state = {"i": 0, "lastread": [(T, 0), (T, 0)]}

        PKV = {}

        def park_bg(kind, src_view, pk_view, fbs):
            for fb in fbs:
                DMA("gpsimd", pk_view[fb][:, 0:8, :],
                    src_view[:, 0:8, fb * 256:(fb + 1) * 256], dPk)
                DMA("gpsimd", pk_view[fb][:, 8:16, :],
                    src_view[:, 8:16, fb * 256:(fb + 1) * 256], dPk)
                parked[(kind, fb)] = dPk.n

        def load_wpiece(kind, fb, src_view, pk_view, pk_base):
            i = wa_state["i"]
            wa_state["i"] = 1 - i
            buf = s_wa[i]
            war = wa_state["lastread"][i]
            dest = buf.ap().rearrange("p (a t) -> p a t", a=NDT)
            key = (kind, fb)
            if key in parked:
                v0 = DMA("sync", dest[:, 0:8, :], pk_view[fb][:, 0:8, :], dW,
                         [war, (dPk, parked[key])])
                v1 = DMA("sync", dest[:, 8:16, :], pk_view[fb][:, 8:16, :], dW)
            else:
                v0 = DMA("gpsimd", dest[:, 0:8, :],
                         src_view[:, 0:8, fb * 256:(fb + 1) * 256], dW, [war])
                v1 = DMA("gpsimd", dest[:, 8:16, :],
                         src_view[:, 8:16, fb * 256:(fb + 1) * 256], dW)
            return buf, dW.n, i

        def wa_mark_read(i, val):
            wa_state["lastread"][i] = (T, val)

        ring_state = {"i": 0, "free": [(V, 0)] * 8}

        def ring_next(allowed=None):
            i = ring_state["i"]
            while allowed is not None and (i % 8) not in allowed:
                i = (i + 1) % 8
            ring_state["i"] = (i + 1) % 8
            return i, RING[i], ring_state["free"][i]

        def ring_set_free(i, sem_val):
            ring_state["free"][i] = sem_val

        x32_state = {"i": 0, "free": [(V, 0), (V, 0)]}
        st_state = {"i": 0, "free": [(T, 0), (T, 0)]}

        guard = {
            "M1": (T, 0), "M2": (T, 0), "M3": (T, 0), "M4": (V, 0), "M5": (dO, 0),
            "kT0": (T, 0), "kT1": (T, 0), "v0": (T, 0), "v1": (T, 0),
            "pt": (T, 0), "attnT_ready": (V, 0), "x1bf_ready": (S, 0),
            "x1f_ready": (V, 0), "q_ready": 0, "kv_ready": 0,
            "gates": (V, 0), "bc": (V, 0),
        }

        def xbf_load(sp):
            m1v = M1.ap().rearrange("p (a t) -> p a t", a=NDT)
            w_m1 = guard["M1"]
            for q4 in range(4):
                DMA("gpsimd", m1v[:, q4 * 4:(q4 + 1) * 4, :],
                    xT_v[:, q4 * 4:(q4 + 1) * 4, sp * C:(sp + 1) * C],
                    dX, [w_m1])
            return dX.n

        def qkv_phase(sp, kv_only, kvslot, extra_vec=None, x_ready=None):
            if x_ready is None:
                x_ready = xbf_load(sp)

            fb_list = list(range(8, 24)) if kv_only else list(range(0, 24))
            for fb in fb_list:
                buf, wv, wi = load_wpiece("qkv", fb, wqkvT_v,
                                          wqkv_pk_v if USE_PARK else None, 0)
                last_t = None
                if fb < 16:
                    for sub in range(2):
                        ftile = fb * 2 + sub
                        ri, rap, rfree = ring_next()
                        for dt in range(NDT):
                            MM(rap, buf[:, dt * 256 + sub * 128: dt * 256 + sub * 128 + 128],
                               M1[:, dt * C:(dt + 1) * C],
                               start=(dt == 0), stop=(dt == NDT - 1),
                               waits=([(dW, wv), (dX, x_ready), rfree]
                                      if dt == 0 else ()),
                               inc=((T, 1) if dt == NDT - 1 else None))
                        t_done = T.n
                        if fb < 8:
                            dst = M2[:, (ftile % NH) * C:(ftile % NH) * C + C]
                            wdst = guard["M2"]
                        else:
                            dst = s_kT[kvslot][:, (ftile - 16) * C:(ftile - 16) * C + C]
                            wdst = guard["kT%d" % kvslot]
                        eng = "vector" if (ftile % 2 == 0) else "scalar"
                        sem = V if eng == "vector" else S
                        if eng == "vector":
                            E(eng, (lambda e, d=dst, p=rap: e.tensor_copy(d, p)),
                              [(T, t_done), wdst], (sem, 1))
                        else:
                            E(eng, (lambda e, d=dst, p=rap: e.copy(d, p)),
                              [(T, t_done), wdst], (sem, 1))
                        ring_set_free(ri, (sem, sem.n))
                        last_t = t_done
                        if extra_vec:
                            extra_vec.pop(0)()
                            if extra_vec and fb % 2 == 0:
                                extra_vec.pop(0)()
                else:
                    fo = (fb - 16) * 256
                    for tt in range(4):
                        ri, rap, rfree = ring_next()
                        for dt in range(NDT):
                            MM(rap[:, 0:256],
                               M1[:, dt * C + tt * 128: dt * C + tt * 128 + 128],
                               buf[:, dt * 256:(dt + 1) * 256],
                               start=(dt == 0), stop=(dt == NDT - 1),
                               waits=([(dW, wv), (dX, x_ready), rfree]
                                      if dt == 0 else ()),
                               inc=((T, 1) if dt == NDT - 1 else None))
                        t_done = T.n
                        dst = s_v[kvslot][:, tt * D + fo: tt * D + fo + 256]
                        eng = "vector" if (tt % 2 == 0) else "scalar"
                        sem = V if eng == "vector" else S
                        if eng == "vector":
                            E(eng, (lambda e, d=dst, p=rap[:, 0:256]: e.tensor_copy(d, p)),
                              [(T, t_done), guard["v%d" % kvslot]], (sem, 1))
                        else:
                            E(eng, (lambda e, d=dst, p=rap[:, 0:256]: e.copy(d, p)),
                              [(T, t_done), guard["v%d" % kvslot]], (sem, 1))
                        ring_set_free(ri, (sem, sem.n))
                        last_t = t_done
                        if extra_vec:
                            extra_vec.pop(0)()
                wa_mark_read(wi, last_t)
            guard["q_ready"] = [(S, S.n), (V, V.n)]
            guard["kv_ready"] = [(S, S.n), (V, V.n)]
            guard["M1_x"] = (T, T.n)
            if kv_only:
                guard["M1"] = (T, T.n)

        def attn_phase(kvslot, is_R, extra_vec):
            arena = {0: pscA, 1: pscB, 2: pscC}
            exp_done = [0, 0, 0]
            pvd_par = [[0, 0, 0], [0, 0, 0]]
            rf = ring_state["free"]
            bankfree = [rf[0], rf[2], rf[4]]
            bankfree2 = [rf[1], rf[3], rf[5]]
            pv_free = [rf[5], rf[7]]
            d_free = rf[6]
            qr = guard["q_ready"]
            norm_par = [0, 0]
            norm_done = 0
            s_bcv_prev = 0
            t_bc_prev = 0
            pts = [s_pt, s_pt2]

            for h in range(NH):
                par = h % 2
                pt = pts[par]
                pvb = [pPV, pX][par]
                vmasks = [0, 0, 0]
                for g, (jts, glo, ghi) in enumerate(GROUPS):
                    for idx, k in enumerate(jts):
                        lo, hi, toff = RLO[k], RHI[k], TOFF[k]
                        w = hi - lo
                        kt = s_kT[1 - kvslot] if k < 4 else s_kT[kvslot]
                        jj = k if k < 4 else k - 4
                        lhs = kt[:, h * C + jj * 128: h * C + jj * 128 + 128]
                        rhs = M2[:, h * C + lo: h * C + hi]
                        waits = []
                        if idx == 0:
                            waits = list(qr) + [bankfree[g], bankfree2[g],
                                                (S, exp_done[g])]
                        MM(arena[g][:, toff - glo: toff - glo + w], lhs, rhs,
                           start=True, stop=True, waits=waits,
                           inc=((T, 1) if idx == len(jts) - 1 else None))
                    t_sc = T.n
                    E("scalar",
                      (lambda e, a=arena[g][:, 0:(ghi - glo)], o=pt[:, glo:ghi]:
                       e.activation(o, a, AF.Exp, bias=EBIAS, scale=SCALE)),
                      [(T, t_sc), (T, pvd_par[par][g])], (S, 1))
                    exp_done[g] = S.n
                    if is_R:
                        E("vector",
                          (lambda e, o=pt[:, glo:ghi], m=M3[:, glo:ghi]:
                           e.tensor_tensor(o, o, m, ALU.mult)),
                          [(S, exp_done[g]), (dX, d_setup)], (V, 1))
                    else:
                        for k in jts:
                            bo = BOFF[k]
                            E("vector",
                              (lambda e, o=pt[:, bo:bo + 128],
                               m=s_maskB[:, k * 128:(k + 1) * 128]:
                               e.tensor_tensor(o, o, m, ALU.mult)),
                              [(S, exp_done[g]), (dX, d_setup)], (V, 1))
                    vmasks[g] = V.n
                for g, (jts, glo, ghi) in enumerate(GROUPS):
                    for k in jts:
                        lo, hi, toff = RLO[k], RHI[k], TOFF[k]
                        w = hi - lo
                        vb = s_v[1 - kvslot] if k < 4 else s_v[kvslot]
                        jj = k if k < 4 else k - 4
                        first = (g == 0 and k == jts[0])
                        lastk = (g == 2 and k == jts[-1])
                        waits = [(V, vmasks[g])]
                        if first:
                            waits += [pv_free[par], d_free, (S, s_bcv_prev)]
                        MM(pvb[:, lo:hi],
                           vb[:, jj * D + h * 128: jj * D + h * 128 + 128],
                           pt[:, toff:toff + w],
                           start=first, stop=lastk, waits=waits)
                        MM(pD[0:1, lo:hi], s_ones1[:, 0:1], pt[:, toff:toff + w],
                           start=first, stop=lastk,
                           inc=(T, 1) if (k == jts[-1]) else None)
                    pvd_par[par][g] = T.n
                t_pvd = T.n
                E("vector", (lambda e: e.reciprocal(s_rd[0:1, :], pD[0:1, :])),
                  [(T, t_pvd), (T, t_bc_prev)], (V, 1))
                v_rd = V.n
                MM(pD[:, 0:C], s_onesr[0:1, :], s_rd[0:1, :], start=True,
                   stop=True, waits=[(V, v_rd), (V, v_ones)], inc=(T, 1))
                t_bc_prev = T.n
                E("scalar", (lambda e: e.copy(s_bc1[:, :], pD[:, 0:C])),
                  [(T, t_bc_prev), (V, norm_done)] +
                  ([x32_state["free"][1]] if h == 0 else []), (S, 1))
                s_bcv_prev = S.n
                d_free = (S, s_bcv_prev)
                E("vector",
                  (lambda e, o=M1[:, h * C:(h + 1) * C], p=pvb:
                   e.tensor_tensor(o, p[:, :], s_bc1[:, :], ALU.mult)),
                  [(S, s_bcv_prev), (T, t_pvd)] +
                  ([guard["M1_x"]] if h == 0 else []),
                  (V, 1))
                norm_done = V.n
                norm_par[par] = V.n
                pv_free[par] = (V, V.n)
                if is_R and debug and h == 0:
                    DMA("sync", dbg["pt"].ap(), s_pt[:, :], dO, [(V, norm_done)])
                    DMA("sync", dbg["rd"].ap(), s_rd[0:1, :], dO, [(V, norm_done)])
                    DMA("sync", dbg["bc1"].ap(), s_bc1[:, :], dO, [(V, norm_done)])
                    dguard = (dO, dO.n)
                    pvd_par = [[0, 0, 0], [0, 0, 0]]
                    exp_done = [0, 0, 0]
                    E("scalar", None, [dguard])
                    E("vector", None, [dguard])
                for _ in range(3):
                    if extra_vec:
                        extra_vec.pop(0)()
            guard["attnT_ready"] = (V, V.n)
            x32_state["free"] = [(V, V.n), (V, V.n)]
            guard["M2"] = (T, T.n)
            guard["kT%d" % (1 - kvslot)] = (T, T.n)
            guard["v%d" % (1 - kvslot)] = (T, T.n)
            guard["kT%d" % kvslot] = (T, T.n)
            guard["v%d" % kvslot] = (T, T.n)
            guard["pt"] = (T, T.n)
            ring_state["free"] = [(S, exp_done[0]), (S, exp_done[0]),
                                  (S, exp_done[1]), (S, exp_done[1]),
                                  (S, exp_done[2]), (V, norm_par[0]),
                                  (S, s_bcv_prev), (V, norm_par[1])]

        def wout_ln1_phase(sp, x1bf_dst, is_R, gkey="M3"):
            atr = guard["attnT_ready"]
            m4w = guard["M4"]
            stat_mm = []
            for fb in range(8):
                buf, wv, wi = load_wpiece("wout", fb, woutT_v,
                                          wout_pk_v if USE_PARK else None, 0)
                last_t = None
                for sub in range(2):
                    fot = fb * 2 + sub
                    xi = x32_state["i"]
                    x32_state["i"] = 1 - xi
                    xv = DMA("scalar", s_x32[xi][:, :],
                             xT_v[:, fot, sp * C:(sp + 1) * C],
                             dX, [x32_state["free"][xi]])
                    ri, rap, rfree = ring_next(allowed=(0, 1, 2, 3, 5, 6))
                    for ft in range(NDT):
                        MM(rap, buf[:, ft * 256 + sub * 128: ft * 256 + sub * 128 + 128],
                           M1[:, ft * C:(ft + 1) * C],
                           start=(ft == 0), stop=(ft == NDT - 1),
                           waits=([(dW, wv), atr, rfree] if ft == 0 else ()),
                           inc=((T, 1) if ft == NDT - 1 else None))
                    t_done = T.n
                    last_t = t_done
                    E("vector",
                      (lambda e, o=M4[:, fot * C:(fot + 1) * C], p=rap,
                       x=s_x32[xi][:, :]:
                       e.scalar_tensor_tensor(o, p, 1.0, x, ALU.bypass, ALU.add)),
                      [(T, t_done), (dX, xv), m4w], (V, 1))
                    v_z1 = V.n
                    x32_state["free"][xi] = (V, v_z1)
                    ring_set_free(ri, (V, v_z1))
                    si = st_state["i"]
                    st_state["i"] = 1 - si
                    E("vector",
                      (lambda e, o=s_sb[si], i_=M4[:, fot * C:(fot + 1) * C]:
                       e.tensor_copy(o, i_)),
                      [st_state["free"][si]], (V, 1))
                    v_sb = V.n
                    E("scalar",
                      (lambda e, o=s_sq[si], i_=M4[:, fot * C:(fot + 1) * C]:
                       e.activation(o, i_, AF.Square)),
                      [(V, v_z1), st_state["free"][si]] +
                      ([guard["pt"]] if fot == 0 else []), (S, 1))
                    s_sqv = S.n
                    w0 = [(V, v_sb)]
                    w1 = [(S, s_sqv)]
                    if fot == 0:
                        w0 += [ring_state["free"][4], (V, v_ones)]
                        w1 += [ring_state["free"][7]]
                    MM(pscC[0:1, 0:C], s_ones1[:, 0:1], s_sb[si],
                       start=(fot == 0), stop=(fot == NDT - 1), waits=w0)
                    MM(pX[0:1, 0:C], s_ones1[:, 0:1], s_sq[si],
                       start=(fot == 0), stop=(fot == NDT - 1), waits=w1,
                       inc=((T, 1) if fot == NDT - 1 else None))
                    st_state["free"][si] = (T, T.n)
                wa_mark_read(wi, last_t)
            t_stat = T.n
            E("vector", (lambda e: e.tensor_scalar_mul(s_bc0[0:1, 0:C],
                                                       pscC[0:1, 0:C], INV_D)),
              [(T, t_stat)], (V, 1))
            E("vector", (lambda e: e.tensor_scalar_mul(s_bc1[0:1, 0:C],
                                                       pX[0:1, 0:C], INV_D)),
              [(V, V.n)], (V, 1))
            E("vector", (lambda e: e.tensor_tensor(s_rd[0:1, 0:C],
                                                   s_bc0[0:1, 0:C],
                                                   s_bc0[0:1, 0:C], ALU.mult)),
              [(V, V.n)], (V, 1))
            E("vector", (lambda e: e.tensor_tensor(s_bc1[0:1, 0:C],
                                                   s_bc1[0:1, 0:C],
                                                   s_rd[0:1, 0:C],
                                                   ALU.subtract)),
              [(V, V.n)], (V, 1))
            v_var = V.n
            E("scalar", (lambda e: e.activation(s_rd[0:1, 0:C],
                                                s_bc1[0:1, 0:C],
                                                AF.Ln, bias=EPS)),
              [(V, v_var)], (S, 1))
            E("scalar", (lambda e: e.activation(s_bc1[0:1, 0:C],
                                                s_rd[0:1, 0:C],
                                                AF.Exp, scale=-0.5)),
              (), (S, 1))
            s_rstd = S.n
            MM(pscC[:, 0:C], s_onesr[0:1, :], s_bc0[0:1, 0:C], start=True,
               stop=True, waits=[(S, s_rstd)], inc=(T, 1))
            MM(pX[:, 0:C], s_onesr[0:1, :], s_bc1[0:1, 0:C], start=True,
               stop=True, inc=(T, 1))
            t_bc = T.n
            E("scalar", (lambda e: e.copy(s_bc0[:, :], pscC[:, 0:C])),
              [(T, t_bc)], (S, 1))
            E("scalar", (lambda e: e.copy(s_bc1[:, :], pX[:, 0:C])), (), (S, 1))
            s_bcv = S.n
            ring_state["free"][4] = (S, s_bcv)
            ring_state["free"][7] = (S, s_bcv)
            x1bf_w = guard[gkey]
            x1bf_vals = []
            for dt in range(NDT):
                E("vector",
                  (lambda e, z=M4[:, dt * C:(dt + 1) * C]:
                   e.tensor_tensor(z, z, s_bc0[:, :], ALU.subtract)),
                  [(S, s_bcv)] if dt == 0 else (), (V, 1))
                E("vector",
                  (lambda e, z=M4[:, dt * C:(dt + 1) * C]:
                   e.tensor_tensor(z, z, s_bc1[:, :], ALU.mult)),
                  (), (V, 1))
                v_ap = V.n
                E("scalar",
                  (lambda e, o=x1bf_dst[:, dt * C:(dt + 1) * C],
                   i_=M4[:, dt * C:(dt + 1) * C]: e.copy(o, i_)),
                  [(V, v_ap)] + ([x1bf_w] if dt == 0 else []), (S, 1))
                x1bf_vals.append(S.n)
            guard["x1f_ready"] = (V, V.n)
            guard["x1bf_ready"] = (S, S.n)
            guard["x1bf_vals"] = x1bf_vals
            guard["M1"] = (T, T.n)
            x32_state["free"] = [(V, V.n), (V, V.n)]

        def router_build(x1bf_src):
            state = {"vfree": [(V, 0), (V, 0)]}
            NPC = 64

            def queue_dma(pi):
                e_idx, q = pi // 8, pi % 8
                bi = pi % 2
                sv = s_sws[bi].ap().rearrange("p (a s) -> p a s", a=2)
                return DMA("gpsimd", sv,
                           swT_v[e_idx][:, q * 2:(q + 1) * 2, :],
                           dR, [state["vfree"][bi]])

            dmavals = [0] * NPC
            dmavals[0] = queue_dma(0)
            dmavals[1] = queue_dma(1)

            def make_piece(pi):
                e_idx, q = pi // 8, pi % 8
                bi = pi % 2

                def emit():
                    for j in range(2):
                        dtg = q * 2 + j
                        E("vector",
                          (lambda e, sw=s_sws[bi][:, j * C:(j + 1) * C],
                           x=x1bf_src[:, dtg * C:(dtg + 1) * C],
                           a=s_acc[:, dtg:dtg + 1]:
                           e.scalar_tensor_tensor(s_sb[0], sw, 1.0, x,
                                                  ALU.bypass, ALU.mult,
                                                  accum_out=a)),
                          [(dR, dmavals[pi]), guard["x1bf_ready"]] if j == 0
                          else (), (V, 1))
                    state["vfree"][bi] = (V, V.n)
                    if q == 7:
                        E("vector",
                          (lambda e, o=s_accE[:, e_idx:e_idx + 1]:
                           e.reduce_sum(o, s_acc[:, 0:16], axis=AX)),
                          [(V, V.n)], (V, 1))
                    if pi + 2 < NPC:
                        dmavals[pi + 2] = queue_dma(pi + 2)
                return emit

            return [make_piece(i) for i in range(NPC)]

        def router_finalize_a():
            v_acc = V.n
            MM(pX[0:1, 0:8], s_ones1f[:, 0:1], s_accE[:, 0:8], start=True,
               stop=True, waits=[(V, v_acc), ring_state["free"][7]], inc=(T, 1))
            t_par = T.n
            E("vector", (lambda e: e.tensor_scalar(s_contrib[0:1, 0:8],
                                                   pX[0:1, 0:8],
                                                   s_bsel[0:1, 0:1], None,
                                                   ALU.mult)),
              [(T, t_par), (V, V.n)], (V, 1))
            E("vector", (lambda e: e.tensor_scalar(s_contrib[0:1, 8:16],
                                                   pX[0:1, 0:8],
                                                   s_bsel[0:1, 1:2], None,
                                                   ALU.mult)),
              [(V, V.n)], (V, 1))
            v_ct = V.n
            ring_state["free"][7] = (V, v_ct)
            DMA("gpsimd", cin.ap(), s_contrib[0:1, :], dR, [(V, v_ct)])
            d_ct = dR.n
            E("gpsimd",
              (lambda e: e.collective_compute(
                  "AllReduce", ALU.add, replica_groups=[list(range(8))],
                  ins=[cin.ap()], outs=[cout.ap()]).then_inc(CCs.h)),
              [(dR, d_ct)], None)
            CCs.n = max(CCs.n, 1)
            DMA("sync", s_lr[0:1, :], cout.ap(), dR, [(CCs, 1)])
            d_lr = dR.n
            E("vector", (lambda e: e.tensor_scalar(s_g[0:1, 0:8], s_lr[0:1, 0:8],
                                                   s_bsel[0:1, 0:1], None,
                                                   ALU.mult)),
              [(dR, d_lr), (V, V.n)], (V, 1))
            E("vector", (lambda e: e.scalar_tensor_tensor(s_g[0:1, 0:8],
                                                          s_lr[0:1, 8:16],
                                                          s_bsel[0:1, 1:2],
                                                          s_g[0:1, 0:8],
                                                          ALU.mult, ALU.add)),
              [(V, V.n)], (V, 1))
            E("vector", (lambda e: e.tensor_tensor(s_g[0:1, 0:8], s_g[0:1, 0:8],
                                                   s_swb[0:1, 0:8], ALU.add)),
              [(V, V.n)], (V, 1))
            E("vector", (lambda e: e.reduce_max(s_g[0:1, 8:9], s_g[0:1, 0:8],
                                                axis=AX)),
              [(V, V.n)], (V, 1))
            E("vector", (lambda e: e.tensor_scalar_mul(s_g[0:1, 9:10],
                                                       s_g[0:1, 8:9], -1.0)),
              [(V, V.n)], (V, 1))
            v_gm = V.n
            E("scalar", (lambda e: e.activation(s_g[0:1, 16:24], s_g[0:1, 0:8],
                                                AF.Exp, bias=s_g[0:1, 9:10])),
              [(V, v_gm)], (S, 1))
            s_ex = S.n
            E("vector", (lambda e: e.reduce_sum(s_g[0:1, 24:25], s_g[0:1, 16:24],
                                                axis=AX)),
              [(S, s_ex), (V, V.n)], (V, 1))
            E("vector", (lambda e: e.reciprocal(s_g[0:1, 25:26], s_g[0:1, 24:25])),
              [(V, V.n)], (V, 1))
            E("vector", (lambda e: e.tensor_scalar(s_g[0:1, 16:24], s_g[0:1, 16:24],
                                                   s_g[0:1, 25:26], None,
                                                   ALU.mult)),
              [(V, V.n)], (V, 1))
            E("vector", (lambda e: e.reduce_max(s_g[0:1, 26:27], s_g[0:1, 16:24],
                                                axis=AX)),
              [(V, V.n)], (V, 1))
            E("vector", (lambda e: e.tensor_scalar(s_g[0:1, 32:40], s_g[0:1, 16:24],
                                                   s_g[0:1, 26:27], None,
                                                   ALU.is_ge)),
              [(V, V.n)], (V, 1))
            E("vector", (lambda e: e.scalar_tensor_tensor(s_g[0:1, 48:56],
                                                          s_g[0:1, 32:40], -2.0,
                                                          s_g[0:1, 16:24],
                                                          ALU.mult, ALU.add)),
              [(V, V.n)], (V, 1))
            E("vector", (lambda e: e.reduce_max(s_g[0:1, 27:28], s_g[0:1, 48:56],
                                                axis=AX)),
              [(V, V.n)], (V, 1))
            E("vector", (lambda e: e.tensor_scalar(s_g[0:1, 40:48], s_g[0:1, 48:56],
                                                   s_g[0:1, 27:28], None,
                                                   ALU.is_ge)),
              [(V, V.n)], (V, 1))
            E("vector", (lambda e: e.tensor_scalar(s_g[0:1, 48:56], s_g[0:1, 32:40],
                                                   s_g[0:1, 26:27], None,
                                                   ALU.mult)),
              [(V, V.n)], (V, 1))
            E("vector", (lambda e: e.tensor_scalar(s_g[0:1, 56:64], s_g[0:1, 40:48],
                                                   s_g[0:1, 27:28], None,
                                                   ALU.mult)),
              [(V, V.n)], (V, 1))
            guard["v_rows"] = V.n

        def router_finalize_b():
            v_rows = guard["v_rows"]
            MM(pX[:, 0:32], s_onesr[0:1, :], s_g[0:1, 32:64], start=True,
               stop=True, waits=[(V, v_rows), ring_state["free"][7]], inc=(T, 1))
            t_ohb = T.n
            E("scalar", (lambda e: e.copy(s_ohb[:, :], pX[:, 0:32])),
              [(T, t_ohb)], (S, 1))
            s_ohbv = S.n
            ring_state["free"][7] = (S, s_ohbv)
            for k in range(2):
                ohcol = (0 if k == 0 else 8)
                ggcol = (16 if k == 0 else 24)
                for e_ in range(8):
                    if e_ == 0:
                        E("vector",
                          (lambda e, o=s_W1s[:, k * 128:(k + 1) * 128],
                           w=s_w1T[:, e_ * 128:(e_ + 1) * 128],
                           c=s_ohb[:, ohcol + e_:ohcol + e_ + 1]:
                           e.tensor_scalar(o, w, c, None, ALU.mult)),
                          [(S, s_ohbv), (dW, dw_setup), (V, V.n)], (V, 1))
                        E("vector",
                          (lambda e, o=s_W2s[:, k * 128:(k + 1) * 128],
                           w=s_w2T[:, e_ * 128:(e_ + 1) * 128],
                           c=s_ohb[:, ggcol + e_:ggcol + e_ + 1]:
                           e.tensor_scalar(o, w, c, None, ALU.mult)),
                          [(V, V.n)], (V, 1))
                        E("vector",
                          (lambda e, o=s_bsel1[:, k:k + 1],
                           w=s_b1T[:, 0:1], c=s_ohb[:, ohcol:ohcol + 1]:
                           e.tensor_scalar(o, w, c, None, ALU.mult)),
                          [(V, V.n)], (V, 1))
                    else:
                        E("vector",
                          (lambda e, o=s_W1s[:, k * 128:(k + 1) * 128],
                           w=s_w1T[:, e_ * 128:(e_ + 1) * 128],
                           c=s_ohb[:, ohcol + e_:ohcol + e_ + 1]:
                           e.scalar_tensor_tensor(o, w, c, o, ALU.mult, ALU.add)),
                          [(V, V.n)], (V, 1))
                        E("vector",
                          (lambda e, o=s_W2s[:, k * 128:(k + 1) * 128],
                           w=s_w2T[:, e_ * 128:(e_ + 1) * 128],
                           c=s_ohb[:, ggcol + e_:ggcol + e_ + 1]:
                           e.scalar_tensor_tensor(o, w, c, o, ALU.mult, ALU.add)),
                          [(V, V.n)], (V, 1))
                        E("vector",
                          (lambda e, o=s_bsel1[:, k:k + 1],
                           w=s_b1T[:, e_:e_ + 1],
                           c=s_ohb[:, ohcol + e_:ohcol + e_ + 1]:
                           e.scalar_tensor_tensor(o, w, c, o, ALU.mult, ALU.add)),
                          [(V, V.n)], (V, 1))
            for e_ in range(8):
                if e_ == 0:
                    E("vector",
                      (lambda e, o=s_bsel1[:, 2:3], w=s_b2T[:, 0:1],
                       c=s_ohb[:, 16:17]:
                       e.tensor_scalar(o, w, c, None, ALU.mult)),
                      [(V, V.n)], (V, 1))
                else:
                    E("vector",
                      (lambda e, o=s_bsel1[:, 2:3], w=s_b2T[:, e_:e_ + 1],
                       c=s_ohb[:, 16 + e_:17 + e_]:
                       e.scalar_tensor_tensor(o, w, c, o, ALU.mult, ALU.add)),
                      [(V, V.n)], (V, 1))
                E("vector",
                  (lambda e, o=s_bsel1[:, 2:3], w=s_b2T[:, e_:e_ + 1],
                   c=s_ohb[:, 24 + e_:25 + e_]:
                   e.scalar_tensor_tensor(o, w, c, o, ALU.mult, ALU.add)),
                  [(V, V.n)], (V, 1))
            guard["gates"] = (V, V.n)
            if debug:
                DMA("sync", dbg["logits"].ap(), s_g[0:1, 0:64], dO,
                    [(V, V.n)])
                DMA("sync", dbg["acc"].ap()[:, 0:16], s_acc[:, :], dO, [(V, V.n)])
                DMA("sync", dbg["acc"].ap()[:, 16:24], s_accE[:, :], dO, [(V, V.n)])

        def ffn_phase(chunkpos, x1bf_src, gkey="M3"):
            gv = guard["gates"]
            x1bvs = guard.get("x1bf_vals", [guard["x1bf_ready"][1]] * NDT)
            zbf_done = [0] * NDT
            hg = guard.setdefault("hbuf", [(T, 0)])
            for ht in range(NDT):
                ri0, rap0, rf0 = ring_next()
                MM(rap0, s_W1s[:, 0:128], x1bf_src[:, ht * C:(ht + 1) * C],
                   start=True, stop=True,
                   waits=[gv, (S, x1bvs[ht]), rf0], inc=(T, 1))
                t0 = T.n
                ri1, rap1, rf1 = ring_next()
                MM(rap1, s_W1s[:, 128:256], x1bf_src[:, ht * C:(ht + 1) * C],
                   start=True, stop=True, waits=[rf1], inc=(T, 1))
                t1 = T.n
                h0 = s_h[:, 0:C]
                h1 = s_h[:, C:2 * C]
                E("scalar",
                  (lambda e, o=h0, p=rap0:
                   e.activation(o, p, AF.Gelu, bias=s_bsel1[:, 0:1])),
                  [(T, t0), hg[0]], (S, 1))
                E("scalar",
                  (lambda e, o=h1, p=rap1:
                   e.activation(o, p, AF.Gelu, bias=s_bsel1[:, 1:2])),
                  [(T, t1)], (S, 1))
                s_h_v = S.n
                ri2, rap2, rf2 = ring_next()
                MM(rap2, s_W2s[:, 0:128], h0, start=True, stop=False,
                   waits=[(S, s_h_v), rf2])
                MM(rap2, s_W2s[:, 128:256], h1, start=False, stop=True,
                   inc=(T, 1))
                t2 = T.n
                hg[0] = (T, t2)
                E("vector",
                  (lambda e, o=x1bf_src[:, ht * C:(ht + 1) * C], p=rap2,
                   x=x1bf_src[:, ht * C:(ht + 1) * C]:
                   e.scalar_tensor_tensor(o, p, s_bsel1[:, 2:3], x,
                                          ALU.add, ALU.add)),
                  [(T, t2)], (V, 1))
                zbf_done[ht] = V.n
                ring_set_free(ri0, (S, s_h_v))
                ring_set_free(ri1, (S, s_h_v))
                ring_set_free(ri2, (V, V.n))
            for half in range(2):
                stat_mm = []
                m5w = guard["M5"]
                for fb in range(8):
                    buf, wv, wi = load_wpiece("w3", fb, w3T_v,
                                              w3_pk_v if USE_PARK else None, 0)
                    last_t = None
                    for sub in range(2):
                        fot = fb * 2 + sub
                        ri, rap, rfree = ring_next(allowed=(0, 1, 2, 3, 5, 6))
                        for dt in range(NDT):
                            MM(rap[:, 0:256],
                               buf[:, dt * 256 + sub * 128: dt * 256 + sub * 128 + 128],
                               x1bf_src[:, dt * C + half * 256:
                                        dt * C + half * 256 + 256],
                               start=(dt == 0), stop=(dt == NDT - 1),
                               waits=([(dW, wv), (V, zbf_done[dt]), rfree]
                                      if dt == 0 else [(V, zbf_done[dt])]),
                               inc=((T, 1) if dt == NDT - 1 else None))
                        t_done = T.n
                        last_t = t_done
                        E("vector",
                          (lambda e, o=M5[:, fot * 256:(fot + 1) * 256],
                           p=rap[:, 0:256],
                           x=M4[:, fot * C + half * 256: fot * C + half * 256 + 256],
                           b=s_b3T[:, fot:fot + 1]:
                           e.scalar_tensor_tensor(o, p, b, x, ALU.add, ALU.add)),
                          [(T, t_done), guard["x1f_ready"]] +
                          ([m5w] if fot == 0 else []), (V, 1))
                        v_z2 = V.n
                        ring_set_free(ri, (V, v_z2))
                        si = st_state["i"]
                        st_state["i"] = 1 - si
                        E("vector",
                          (lambda e, o=s_sb_q[si],
                           i_=M5[:, fot * 256:(fot + 1) * 256]:
                           e.tensor_copy(o, i_)),
                          [st_state["free"][si]], (V, 1))
                        v_sb = V.n
                        E("scalar",
                          (lambda e, o=s_sq_q[si],
                           i_=M5[:, fot * 256:(fot + 1) * 256]:
                           e.activation(o, i_, AF.Square)),
                          [(V, v_z2), st_state["free"][si]], (S, 1))
                        s_sqv = S.n
                        w0 = [(V, v_sb)]
                        w1 = [(S, s_sqv)]
                        if fot == 0:
                            w0 += [ring_state["free"][4]]
                            w1 += [ring_state["free"][7]]
                        MM(pscC[0:1, 0:256], s_ones1[:, 0:1], s_sb_q[si],
                           start=(fot == 0), stop=(fot == NDT - 1), waits=w0)
                        MM(pX[0:1, 0:256], s_ones1[:, 0:1], s_sq_q[si],
                           start=(fot == 0), stop=(fot == NDT - 1), waits=w1,
                           inc=((T, 1) if fot == NDT - 1 else None))
                        st_state["free"][si] = (T, T.n)
                    wa_mark_read(wi, last_t)
                t_stat = T.n
                E("vector", (lambda e: e.tensor_scalar_mul(s_bc0[0:1, 0:256],
                                                           pscC[0:1, 0:256],
                                                           INV_D)),
                  [(T, t_stat)], (V, 1))
                E("vector", (lambda e: e.tensor_scalar_mul(s_bc1[0:1, 0:256],
                                                           pX[0:1, 0:256], INV_D)),
                  [(V, V.n)], (V, 1))
                E("vector", (lambda e: e.tensor_tensor(s_rd[0:1, 0:256],
                                                       s_bc0[0:1, 0:256],
                                                       s_bc0[0:1, 0:256],
                                                       ALU.mult)),
                  [(V, V.n)], (V, 1))
                E("vector", (lambda e: e.tensor_tensor(s_bc1[0:1, 0:256],
                                                       s_bc1[0:1, 0:256],
                                                       s_rd[0:1, 0:256],
                                                       ALU.subtract)),
                  [(V, V.n)], (V, 1))
                v_var = V.n
                E("scalar", (lambda e: e.activation(s_rd[0:1, 0:256],
                                                    s_bc1[0:1, 0:256],
                                                    AF.Ln, bias=EPS)),
                  [(V, v_var)], (S, 1))
                E("scalar", (lambda e: e.activation(s_bc1[0:1, 0:256],
                                                    s_rd[0:1, 0:256],
                                                    AF.Exp, scale=-0.5)),
                  (), (S, 1))
                s_rstd = S.n
                MM(pscC[:, 0:256], s_onesr[0:1, :], s_bc0[0:1, 0:256],
                   start=True, stop=True, waits=[(S, s_rstd)], inc=(T, 1))
                MM(pX[:, 0:256], s_onesr[0:1, :], s_bc1[0:1, 0:256],
                   start=True, stop=True, inc=(T, 1))
                t_bc = T.n
                E("scalar", (lambda e: e.copy(s_bc0[:, 0:256], pscC[:, 0:256])),
                  [(T, t_bc)], (S, 1))
                E("scalar", (lambda e: e.copy(s_bc1[:, 0:256], pX[:, 0:256])),
                  (), (S, 1))
                s_bcv = S.n
                ring_state["free"][4] = (S, s_bcv)
                ring_state["free"][7] = (S, s_bcv)
                for dt in range(NDT):
                    E("vector",
                      (lambda e, z=M5[:, dt * 256:(dt + 1) * 256]:
                       e.tensor_tensor(z, z, s_bc0[:, 0:256], ALU.subtract)),
                      [(S, s_bcv)] if dt == 0 else (), (V, 1))
                    E("vector",
                      (lambda e, z=M5[:, dt * 256:(dt + 1) * 256]:
                       e.tensor_tensor(z, z, s_bc1[:, 0:256], ALU.mult)),
                      (), (V, 1))
                v_y = V.n
                x32_state["free"] = [(V, v_y), (V, v_y)]
                m5view = M5.ap().rearrange("p (a t) -> p a t", a=NDT)
                DMA("scalar", yT_v[:, 0:8, chunkpos * C + half * 256:
                                 chunkpos * C + half * 256 + 256],
                    m5view[:, 0:8, :], dO, [(V, v_y)])
                DMA("scalar", yT_v[:, 8:16, chunkpos * C + half * 256:
                                 chunkpos * C + half * 256 + 256],
                    m5view[:, 8:16, :], dO)
                guard["M5"] = (dO, dO.n)
            guard["M4"] = (V, V.n)
            guard[gkey] = (T, T.n)

        def _schedule(ck):
            qkv_phase(0, kv_only=True, kvslot=0)
            ck()
            qkv_phase(1, kv_only=False, kvslot=1)
            if debug:
                DMA("sync", dbg["q"].ap(), M2[:, :], dO, list(guard["q_ready"]))
                guard["M2"] = (dO, dO.n)
                DMA("sync", dbg["kT"].ap(), s_kT[1][:, :], dO, list(guard["kv_ready"]))
                guard["kT1"] = (dO, dO.n)
                DMA("sync", dbg["v"].ap(), s_v[1][:, :], dO, list(guard["kv_ready"]))
                guard["v1"] = (dO, dO.n)
            ck()
            attn_phase(kvslot=1, is_R=True, extra_vec=[])
            ck()
            guard["M3"] = (V, V.n)
            wout_ln1_phase(1, M3, is_R=True, gkey="M3")
            ck()
            if debug:
                DMA("sync", dbg["x1f"].ap(), M4[:, :], dO, [guard["x1f_ready"]])
                guard["M4"] = (dO, dO.n)
                DMA("sync", dbg["attnT"].ap(), M1[:, :], dO, [guard["attnT_ready"]])
                guard["M1"] = (dO, dO.n)
            park_bg("qkv", wqkvT_v, wqkv_pk_v, range(8, 24))
            x_h2 = xbf_load(2)
            park_bg("qkv", wqkvT_v, wqkv_pk_v, range(0, 8))
            park_bg("w3", w3T_v, w3_pk_v, range(0, 8))
            rpieces = router_build(M3)
            qkv_phase(2, kv_only=True, kvslot=0, x_ready=x_h2)
            ck()
            x_a0 = xbf_load(3)
            park_bg("wout", woutT_v, wout_pk_v, range(0, 8))
            qkv_phase(3, kv_only=False, kvslot=1, extra_vec=rpieces,
                      x_ready=x_a0)
            ck()
            attn_phase(kvslot=1, is_R=False, extra_vec=rpieces)
            ck()
            while rpieces:
                rpieces.pop(0)()
            router_finalize_a()
            router_finalize_b()
            ck()
            ffn_phase(0, M3, gkey="M3")
            ck()
            wout_ln1_phase(3, M2, is_R=False, gkey="M2")
            ck()
            ffn_phase(1, M2, gkey="M2")
            ck()
            qkv_phase(4, kv_only=False, kvslot=0)
            ck()
            attn_phase(kvslot=0, is_R=False, extra_vec=[])
            ck()
            wout_ln1_phase(4, M2, is_R=False, gkey="M2")
            ck()
            ffn_phase(2, M2, gkey="M2")
            ck()
            qkv_phase(5, kv_only=False, kvslot=1)
            ck()
            attn_phase(kvslot=1, is_R=False, extra_vec=[])
            ck()
            wout_ln1_phase(5, M2, is_R=False, gkey="M2")
            ck()
            ffn_phase(3, M2, gkey="M2")
            ck()

        class _Stop(Exception):
            pass

        _pc = [0]

        def ck():
            _pc[0] += 1
            if _pc[0] > nphase:
                raise _Stop()

        try:
            _schedule(ck)
        except _Stop:
            pass

        E("sync", None, [(dO, dO.n)])

        build_program._events = events
        block = st_.enter_context(nc.Block())

        def replay(name):
            def go(eng):
                lastw = {}
                for (en, waits, fn, incs) in events:
                    if en != name:
                        continue
                    for (h, v_) in waits:
                        if v_ > lastw.get(h, 0):
                            eng.wait_ge(h, v_)
                            lastw[h] = v_
                    if fn is not None:
                        ins = fn(eng)
                        for (h, by) in incs:
                            ins = ins.then_inc(h, by)
            return go

        block.tensor(replay("tensor"))
        block.scalar(replay("scalar"))
        block.vector(replay("vector"))
        block.gpsimd(replay("gpsimd"))
        block.sync(replay("sync"))

    return nc


def _band_mask(zero_prev):
    m = np.zeros((128, 2560), dtype=np.float32)
    for k in range(8):
        lo, hi, toff = RLO[k], RHI[k], TOFF[k]
        p = np.arange(128)[:, None]
        cc = np.arange(lo, hi)[None, :]
        j = 128 * k + p
        if k < 4:
            v = (cc <= j).astype(np.float32)
            if zero_prev:
                v[:] = 0.0
        else:
            v = (cc >= j - 512).astype(np.float32)
        m[:, toff:toff + hi - lo] = v
    return m.astype(ml_dtypes.bfloat16)


def _boundary_mask():
    m = np.zeros((128, 1024), dtype=np.float32)
    for k in range(8):
        p = np.arange(128)[:, None]
        cc = (BLO[k] + np.arange(128))[None, :]
        j = 128 * k + p
        if k < 4:
            v = (cc <= j).astype(np.float32)
        else:
            v = (cc >= j - 512).astype(np.float32)
        m[:, k * 128:(k + 1) * 128] = v
    return m.astype(ml_dtypes.bfloat16)


def make_inputs(c, L0_attn_x, L0_attn_w_qkv, L0_attn_w_out, L0_ffn_switch_w,
                L0_ffn_switch_b, L0_ffn_w1, L0_ffn_b1, L0_ffn_w2, L0_ffn_b2,
                L0_ffn_w3, L0_ffn_b3, shared):
    b, ct = c // 4, c % 4
    xb = np.asarray(L0_attn_x[b])
    seq = [ct - 1, ct, 3 * ct + 3, 3 * ct + 4, 3 * ct + 5, 3 * ct + 6]
    xTm = np.zeros((D, 6 * C), dtype=np.float32)
    for i, ch in enumerate(seq):
        if ch >= 0:
            xTm[:, i * C:(i + 1) * C] = xb[ch * C:(ch + 1) * C, :].T
    sw = np.asarray(L0_ffn_switch_w).reshape(8, 2048, 2048)
    swTm = np.ascontiguousarray(
        sw[:, ct * C:(ct + 1) * C, :].transpose(0, 2, 1)).reshape(8 * D, C)
    bselr = np.zeros((1, 2), dtype=np.float32)
    bselr[0, b] = 1.0
    d = dict(shared)
    d.update({
        "xT": np.ascontiguousarray(xTm),
        "swT": swTm.astype(np.float32),
        "bsel": bselr,
        "maskR": np.ascontiguousarray(_band_mask(zero_prev=(ct == 0))),
    })
    return d


def kernel(L0_attn_x, L0_attn_w_qkv, L0_attn_w_out, L0_ffn_switch_w,
           L0_ffn_switch_b, L0_ffn_w1, L0_ffn_b1, L0_ffn_w2, L0_ffn_b2,
           L0_ffn_w3, L0_ffn_b3):
    args = (L0_attn_x, L0_attn_w_qkv, L0_attn_w_out, L0_ffn_switch_w,
            L0_ffn_switch_b, L0_ffn_w1, L0_ffn_b1, L0_ffn_w2, L0_ffn_b2,
            L0_ffn_w3, L0_ffn_b3)
    wqkv = np.asarray(L0_attn_w_qkv)
    shared = {
        "wqkvT": np.ascontiguousarray(wqkv.T),
        "woutT": np.ascontiguousarray(np.asarray(L0_attn_w_out).T),
        "w3T": np.ascontiguousarray(np.asarray(L0_ffn_w3).T),
        "w1T": np.ascontiguousarray(
            np.concatenate([np.asarray(L0_ffn_w1)[e].T for e in range(8)], axis=1)),
        "w2T": np.ascontiguousarray(
            np.concatenate([np.asarray(L0_ffn_w2)[e].T for e in range(8)], axis=1)),
        "b1T": np.ascontiguousarray(np.asarray(L0_ffn_b1).T),
        "b2T": np.ascontiguousarray(np.asarray(L0_ffn_b2).T),
        "b3T": np.ascontiguousarray(np.asarray(L0_ffn_b3).reshape(16, 128).T),
        "swb": np.asarray(L0_ffn_switch_b).reshape(1, 8).astype(np.float32),
        "maskB": np.ascontiguousarray(_boundary_mask()),
        "cvals": np.tile(np.array(CONSTS, dtype=np.float32)[None, :], (128, 1)),
    }
    nc = build_program(debug=False)
    in_maps = [make_inputs(c, *args, shared=shared) for c in range(8)]
    res = run_bass_kernel_spmd(nc, in_maps, core_ids=list(range(8)))
    out = np.zeros((2, 8192, 2048), dtype=np.float32)
    for c in range(8):
        b, ct = c // 4, c % 4
        ymat = res.results[c]["yT"]
        owned = [ct, 3 * ct + 4, 3 * ct + 5, 3 * ct + 6]
        for i, ch in enumerate(owned):
            out[b, ch * C:(ch + 1) * C, :] = ymat[:, i * C:(i + 1) * C].T
    return out
